# version 16
# baseline (speedup 1.0000x reference)
import math
from contextlib import ExitStack

import numpy as np
import ml_dtypes
import concourse.bass as bass
import concourse.mybir as mybir
from concourse.bass_utils import run_bass_kernel_spmd

F32 = mybir.dt.float32
BF16 = mybir.dt.bfloat16
AF = mybir.ActivationFunctionType
OP = mybir.AluOpType
AX = mybir.AxisListType
bfnp = ml_dtypes.bfloat16

NE = 8
RL = 16
ER = NE * RL
LORA_SCALING = 2.0


class Sem:
    def __init__(self, nc, ctx, name):
        self.h = ctx.enter_context(nc.semaphore(name))
        self.n = 0

    def inc(self, instr, v=1):
        instr.then_inc(self.h, v)
        self.n += v
        return self.n


def build_program(D, NKV, TOWN, H, detect_races=True, stop_after=None,
                  debug_dumps=False):
    E = 64
    HE = H * E
    HEAD_DIM = D // H
    CCH = HE // 128
    DCH = D // 128
    KCH = NKV // 128
    TT = TOWN // 128
    NQC = max(1, TOWN // 512)
    QCW = TOWN // NQC
    VW = E + 1
    VROW = H * VW
    NG = KCH // 2
    GRP = 4
    assert HE % 128 == 0 and D % 128 == 0 and NKV % 128 == 0 and TOWN % 128 == 0
    assert KCH % 2 == 0 and QCW <= 512 and HE <= 512
    OGW = 4 if DCH % 4 == 0 else DCH
    assert DCH % OGW == 0 and DCH % 2 == 0

    nc = bass.Bass("TRN2", target_bir_lowering=False,
                   detect_race_conditions=detect_races)

    x_dT = nc.dram_tensor("x_dT", [D, TOWN], F32, kind="ExternalInput")
    wqT = nc.dram_tensor("wqT", [D, HE], F32, kind="ExternalInput")
    wkT = nc.dram_tensor("wkT", [D, HE], F32, kind="ExternalInput")
    wvT = nc.dram_tensor("wvT", [D, HE], F32, kind="ExternalInput")
    bq_r = nc.dram_tensor("bq_r", [CCH, 128], F32, kind="ExternalInput")
    bk_r = nc.dram_tensor("bk_r", [CCH, 128], F32, kind="ExternalInput")
    bv_r = nc.dram_tensor("bv_r", [1, HE], F32, kind="ExternalInput")
    woT = nc.dram_tensor("woT", [HE, D], F32, kind="ExternalInput")
    bo_r = nc.dram_tensor("bo_r", [DCH, 128], F32, kind="ExternalInput")
    gw_pk = nc.dram_tensor("gw_pk", [128, NE * DCH], F32, kind="ExternalInput")
    bwT = nc.dram_tensor("bwT", [D, D], F32, kind="ExternalInput")
    bb_r = nc.dram_tensor("bb_r", [DCH, 128], F32, kind="ExternalInput")
    la_pk = nc.dram_tensor("la_pk", [128, ER * DCH], F32, kind="ExternalInput")
    lb_pk = nc.dram_tensor("lb_pk", [ER, D], F32, kind="ExternalInput")
    out_dT = nc.dram_tensor("out_dT", [D, TOWN], F32, kind="ExternalOutput")
    if debug_dumps:
        dbg_OnT = nc.dram_tensor("dbg_OnT", [128, CCH * TOWN], BF16, kind="ExternalOutput")
        dbg_negC = nc.dram_tensor("dbg_negC", [128, H], F32, kind="ExternalOutput")
        dbg_rows = nc.dram_tensor("dbg_rows", [1, 2 * NQC * QCW], F32, kind="ExternalOutput")
        dbg_R = nc.dram_tensor("dbg_R", [128, DCH * TOWN], BF16, kind="ExternalOutput")
        dbg_x1q = nc.dram_tensor("dbg_x1q", [128, DCH * TOWN], BF16, kind="ExternalOutput")
        dbg_wch = nc.dram_tensor("dbg_wch", [NE, QCW], F32, kind="ExternalOutput")
        dbg_low = nc.dram_tensor("dbg_low", [128, QCW], BF16, kind="ExternalOutput")
        dbg_K = nc.dram_tensor("dbg_K", [128, CCH * NKV], BF16, kind="ExternalOutput")
        dbg_gu = nc.dram_tensor("dbg_gu", [NE, QCW], F32, kind="ExternalOutput")
        dbg_glt = nc.dram_tensor("dbg_glt", [128, 8 * 4], F32, kind="ExternalOutput")
        dbg_y2 = nc.dram_tensor("dbg_y2", [128, DCH * TOWN], BF16, kind="ExternalOutput")
        dbg_tc = nc.dram_tensor("dbg_tc", [128, 8], F32, kind="ExternalOutput")
        dbg_x1q0 = nc.dram_tensor("dbg_x1q0", [128, DCH * QCW], BF16, kind="ExternalOutput")
        dbg_V = nc.dram_tensor("dbg_V", [128, KCH * VROW], BF16, kind="ExternalOutput")
        dbg_Q = nc.dram_tensor("dbg_Q", [128, CCH * TOWN], BF16, kind="ExternalOutput")

    ccK_in = nc.dram_tensor("ccK_in", [HE, TOWN], BF16, kind="Internal")
    ccK_out = nc.dram_tensor("ccK_out", [GRP * HE, TOWN], BF16, kind="Internal")
    ccV_in = nc.dram_tensor("ccV_in", [TOWN, VROW], BF16, kind="Internal")
    ccV_out = nc.dram_tensor("ccV_out", [GRP * TOWN, VROW], BF16, kind="Internal")

    ident_bf_d = nc.inline_tensor(np.eye(128, dtype=bfnp), "c_ident_bf")
    ident_f_d = nc.inline_tensor(np.eye(128, dtype=np.float32), "c_ident_f")
    ones_row_d = nc.inline_tensor(np.ones((1, 128), dtype=np.float32), "c_ones_row")
    ones_col_d = nc.inline_tensor(np.ones((128, 1), dtype=bfnp), "c_ones_col")
    sel_d = nc.inline_tensor(
        LORA_SCALING * np.kron(np.eye(NE, dtype=np.float32),
                               np.ones((1, RL), dtype=np.float32)),
        "c_sel_f")

    est = ExitStack()
    with est:
        def sb(ctx, name, p, f, dtype):
            return ctx.enter_context(nc.sbuf_tensor(name, [p, f], dtype))

        ident_bf = sb(est, "ident_bf_s", 128, 128, BF16)
        ident_f = sb(est, "ident_f_s", 128, 128, F32)
        ones_row = sb(est, "ones_row_s", 1, 128, F32)
        ones_col = sb(est, "ones_col_s", 128, 1, BF16)
        sel_f = sb(est, "sel_s", NE, 128, F32)
        bq_sb = sb(est, "bq_sb", 128, CCH, F32)
        bk_sb = sb(est, "bk_sb", 128, CCH, F32)
        bo_sb = sb(est, "bo_sb", 128, DCH, F32)
        bb_sb = sb(est, "bb_sb", 128, DCH, F32)
        negC = sb(est, "negC", 128, H, F32)
        negm_r = sb(est, "negm_r", 1, NQC * QCW, F32)
        inv_r = sb(est, "inv_r", 1, NQC * QCW, F32)
        lnscr = sb(est, "lnscr", 1, 3 * QCW, F32)
        ctxOn = ExitStack()
        OnT = ctxOn.enter_context(
            nc.sbuf_tensor("OnT", [128, CCH * TOWN], BF16, side="right"))
        vb_sb = ctxOn.enter_context(
            nc.sbuf_tensor("vb_sb", [128, HE], F32, side="right"))

        sL = Sem(nc, est, "sL")
        sB = Sem(nc, est, "sB")
        sXr = [Sem(nc, est, f"sXr{i}") for i in range(3)]
        sWr = [Sem(nc, est, f"sWr{i}") for i in range(3)]
        sGa = Sem(nc, est, "sGa")
        sXre = [Sem(nc, est, f"sXre{i}") for i in range(6)]
        sWo = [Sem(nc, est, f"sWo{i}") for i in range(2)]
        sFw = [Sem(nc, est, f"sFw{i}") for i in range(8)]
        sOu = [Sem(nc, est, f"sOu{i}") for i in range(4)]
        sG = Sem(nc, est, "sG")
        sO = Sem(nc, est, "sO")
        sCK = Sem(nc, est, "sCK")
        sCV = Sem(nc, est, "sCV")
        sT = Sem(nc, est, "sT")
        sV = Sem(nc, est, "sV")
        sC = Sem(nc, est, "sC")
        sP = Sem(nc, est, "sP")

        block = est.enter_context(nc.Block())

        def _(q):
            for dst, src in [(ident_bf, ident_bf_d), (ident_f, ident_f_d),
                             (ones_row, ones_row_d), (ones_col, ones_col_d),
                             (sel_f, sel_d)]:
                sB.n += 16
                q.dma_start(out=dst[:, :], in_=src[:, :]).then_inc(sB.h, 16)
        block.sync(_)
        n_const = sB.n

        ps0 = ExitStack()
        ps_bias = ps0.enter_context(nc.psum_tensor("ps_bias", [128, 512], F32))
        braw = ps0.enter_context(nc.sbuf_tensor("bias_raw", [max(DCH, CCH), 128], F32))
        bvraw = ps0.enter_context(nc.sbuf_tensor("bv_raw", [1, HE], F32))

        for i, (src, nch, dst) in enumerate(
                [(bq_r, CCH, bq_sb), (bk_r, CCH, bk_sb),
                 (bo_r, DCH, bo_sb), (bb_r, DCH, bb_sb)]):
            vprev = sV.n

            def _(q, src=src, nch=nch, vprev=vprev, i=i):
                if i > 0:
                    q.wait_ge(sV.h, vprev)
                sB.n += 16
                q.dma_start(out=braw[0:nch, :], in_=src[:, :]).then_inc(sB.h, 16)
            block.sync(_)
            lw = sB.n

            def _(te, nch=nch, lw=lw, vprev=vprev, i=i):
                te.wait_ge(sB.h, lw)
                if i != 0:
                    te.wait_ge(sV.h, vprev)
                sT.inc(te.transpose(ps_bias[0:128, 0:nch], braw[0:nch, 0:128],
                                    ident_f[0:nch, 0:nch]))
            block.tensor(_)
            tw = sT.n

            def _(dv, dst=dst, nch=nch, tw=tw):
                dv.wait_ge(sT.h, tw)
                sV.inc(dv.tensor_copy(out=dst[:, 0:nch], in_=ps_bias[:, 0:nch]))
            block.vector(_)

        vprev = sV.n

        def _(q):
            sB.n += 16
            q.dma_start(out=bvraw[:, :], in_=bv_r[:, :]).then_inc(sB.h, 16)
        block.sync(_)
        lw = sB.n

        def _(te, lw=lw, vprev=vprev):
            te.wait_ge(sB.h, lw)
            te.wait_ge(sV.h, vprev)
            sT.inc(te.matmul(ps_bias[:, 0:HE], ones_row[0:1, :], bvraw[0:1, :],
                             start=True, stop=True))
        block.tensor(_)
        tw = sT.n

        def _(dv, tw=tw):
            dv.wait_ge(sT.h, tw)
            sV.inc(dv.tensor_copy(out=vb_sb[:], in_=ps_bias[:, 0:HE]))
        block.vector(_)
        bias_ready_v = sV.n
        ps0.close()

        ctxA = ExitStack()
        xbf = sb(ctxA, "xbf", 128, DCH * TOWN, BF16)
        K_res = sb(ctxA, "K_res", 128, CCH * NKV, BF16)
        V_res = sb(ctxA, "V_res", 128, KCH * VROW, BF16)
        Q_bf = sb(ctxA, "Q_bf", 128, CCH * TOWN, BF16)
        KV_own = sb(ctxA, "KV_own", 128, max(CCH * TOWN, TT * VROW), BF16)
        XR = 3
        xin = [sb(ctxA, f"xin{i}", 128, TOWN, F32) for i in range(XR)]
        WR = 3
        w_f = [sb(ctxA, f"w_f{i}", 128, HE, F32) for i in range(WR)]
        w_b = [sb(ctxA, f"w_b{i}", 128, HE, BF16) for i in range(WR)]
        P_bf = [sb(ctxA, f"P_bf{i}", 128, 2 * QCW, BF16) for i in range(2)]
        zr_t = sb(ctxA, "zr_t", 1, 2 * QCW, F32)
        zrb_sb = sb(ctxA, "zrb_sb", 64, QCW, F32)
        cmax = sb(ctxA, "cmax", 128, 2, F32)
        cmaxb = sb(ctxA, "cmaxb", 128, 1, BF16)

        psA = ExitStack()
        NPROJ = max(CCH * NQC, TT)
        ps_proj = psA.enter_context(nc.psum_tensor("ps_proj", [128, 512 * NPROJ], F32))

        sP_x0 = sP.n

        def _(q):
            for dc in range(DCH):
                if dc >= XR:
                    q.wait_ge(sP.h, sP_x0 + dc - XR + 1)
                sXr[dc % XR].n += 16
                q.dma_start(out=xin[dc % XR][:, :],
                            in_=x_dT[128 * dc:128 * (dc + 1), :]
                            ).then_inc(sXr[dc % XR].h, 16)
        block.sync(_)

        def _(g):
            for dc in range(DCH):
                g.wait_ge(sXr[dc % XR].h, 16 * (dc // XR + 1))
                sP.inc(g.tensor_copy(out=xbf[:, dc * TOWN:(dc + 1) * TOWN],
                                     in_=xin[dc % XR][:, :]))
        block.gpsimd(_)
        assert sP.n == sP_x0 + DCH
        x_conv = [sP_x0 + dc + 1 for dc in range(DCH)]

        wr_uses = [0, 0, 0]

        def proj_pass(w_dram, ngrp, kind, free_sem, free_cnt, prev_conv_final):
            sWc0 = sP.n
            sT0 = sT.n
            wl = []

            def _(q):
                for dc in range(DCH):
                    if dc >= WR:
                        q.wait_ge(sP.h, sWc0 + dc - WR + 1)
                    elif prev_conv_final is not None:
                        q.wait_ge(sP.h, prev_conv_final)
                    s = dc % WR
                    wr_uses[s] += 1
                    wl.append(16 * wr_uses[s])
                    sWr[s].n += 16
                    q.dma_start(out=w_f[s][:, :],
                                in_=w_dram[128 * dc:128 * (dc + 1), :]
                                ).then_inc(sWr[s].h, 16)
            block.sync(_)

            def _(g):
                for dc in range(DCH):
                    g.wait_ge(sWr[dc % WR].h, wl[dc])
                    if dc >= WR:
                        g.wait_ge(sT.h, sT0 + (dc - WR + 1))
                    sP.inc(g.tensor_copy(out=w_b[dc % WR][:, :], in_=w_f[dc % WR][:, :]))
            block.gpsimd(_)
            assert sP.n == sWc0 + DCH

            grp_cnt = []

            def _(te):
                te.wait_ge(free_sem.h, free_cnt)
                for dc in range(DCH):
                    te.wait_ge(sP.h, sWc0 + dc + 1)
                    te.wait_ge(sP.h, x_conv[dc])
                    last = None
                    for a in range(ngrp):
                        if kind == "ck":
                            cc, qc = divmod(a, NQC)
                            last = te.matmul(
                                ps_proj[:, 512 * a:512 * a + QCW],
                                w_b[dc % WR][:, 128 * cc:128 * (cc + 1)],
                                xbf[:, dc * TOWN + qc * QCW: dc * TOWN + (qc + 1) * QCW],
                                start=(dc == 0), stop=(dc == DCH - 1))
                        else:
                            last = te.matmul(
                                ps_proj[:, 512 * a:512 * a + HE],
                                xbf[:, dc * TOWN + 128 * a: dc * TOWN + 128 * (a + 1)],
                                w_b[dc % WR][:, 0:HE],
                                start=(dc == 0), stop=(dc == DCH - 1))
                        if dc == DCH - 1:
                            grp_cnt.append(sT.inc(last))
                    if dc < DCH - 1:
                        sT.inc(last)
            block.tensor(_)
            assert sT.n == sT0 + (DCH - 1) + ngrp
            return grp_cnt

        kg = proj_pass(wkT, CCH * NQC, "ck", sV, bias_ready_v, None)
        k_conv_final = sP.n

        def _(dv):
            dv.wait_ge(sV.h, bias_ready_v)
            for a in range(CCH * NQC):
                cc, qc = divmod(a, NQC)
                dv.wait_ge(sT.h, kg[a])
                sV.inc(dv.tensor_scalar(
                    KV_own[:, cc * TOWN + qc * QCW: cc * TOWN + (qc + 1) * QCW],
                    ps_proj[:, 512 * a:512 * a + QCW],
                    bk_sb[:, cc:cc + 1], None, OP.add))
        block.vector(_)
        kev_v = sV.n

        def _(g):
            g.wait_ge(sV.h, kev_v)
            for cc in range(CCH):
                sG.n += 16
                g.dma_start(out=ccK_in[128 * cc:128 * (cc + 1), :],
                            in_=KV_own[:, cc * TOWN:(cc + 1) * TOWN]).then_inc(sG.h, 16)
            g.wait_ge(sG.h, sG.n)
            sCK.inc(g.collective_compute(
                "AllGather", OP.bypass,
                replica_groups=[[0, 1, 2, 3], [4, 5, 6, 7]],
                ins=[ccK_in.ap().opt()], outs=[ccK_out.ap().opt()]))
        block.gpsimd(_)

        vg = proj_pass(wvT, TT, "v", sV, kev_v, k_conv_final)
        v_conv_final = sP.n

        ccK_dma_g = sG.n

        def _(dv):
            dv.wait_ge(sV.h, bias_ready_v)
            dv.wait_ge(sG.h, ccK_dma_g)
            for tt in range(TT):
                dv.wait_ge(sT.h, vg[tt])
                last = None
                for h in range(H):
                    last = dv.tensor_add(
                        KV_own[:, tt * VROW + VW * h: tt * VROW + VW * h + E],
                        ps_proj[:, 512 * tt + E * h: 512 * tt + E * (h + 1)],
                        vb_sb[:, E * h:E * (h + 1)])
                sV.inc(last)
        block.vector(_)
        sV_V = sV.n

        def _(g):
            g.wait_ge(sV.h, sV_V)
            last = None
            for tt in range(TT):
                for h in range(H):
                    last = g.memset(
                        KV_own[:, tt * VROW + VW * h + E: tt * VROW + VW * (h + 1)], 1.0)
            sP.inc(last)
            for tt in range(TT):
                sG.n += 16
                g.dma_start(out=ccV_in[128 * tt:128 * (tt + 1), :],
                            in_=KV_own[:, tt * VROW:(tt + 1) * VROW]).then_inc(sG.h, 16)
            g.wait_ge(sG.h, sG.n)
            sCV.inc(g.collective_compute(
                "AllGather", OP.bypass,
                replica_groups=[[0, 1, 2, 3], [4, 5, 6, 7]],
                ins=[ccV_in.ap().opt()], outs=[ccV_out.ap().opt()]))
        block.gpsimd(_)

        qg = proj_pass(wqT, CCH * NQC, "ck", sV, sV_V, v_conv_final)

        def _(dv):
            for a in range(CCH * NQC):
                cc, qc = divmod(a, NQC)
                dv.wait_ge(sT.h, qg[a])
                sV.inc(dv.tensor_scalar(
                    Q_bf[:, cc * TOWN + qc * QCW: cc * TOWN + (qc + 1) * QCW],
                    ps_proj[:, 512 * a:512 * a + QCW],
                    bq_sb[:, cc:cc + 1], None, OP.add))
        block.vector(_)
        qev_v = sV.n

        def _(q):
            q.wait_ge(sCK.h, 1)
            for cc in range(CCH):
                for r in range(GRP):
                    sGa.n += 16
                    q.dma_start(
                        out=K_res[:, cc * NKV + TOWN * r: cc * NKV + TOWN * (r + 1)],
                        in_=ccK_out[HE * r + 128 * cc: HE * r + 128 * (cc + 1), :]
                    ).then_inc(sGa.h, 16)
            q.wait_ge(sCV.h, 1)
            for kt in range(KCH):
                sGa.n += 16
                q.dma_start(out=V_res[:, kt * VROW:(kt + 1) * VROW],
                            in_=ccV_out[128 * kt:128 * (kt + 1), :]).then_inc(sGa.h, 16)
        block.sync(_)
        sL_gather = sGa.n

        if stop_after == "proj":
            psA.close()
            ctxA.close()
            return nc

        if debug_dumps:
            def _(g):
                g.wait_ge(sGa.h, sL_gather)
                g.wait_ge(sV.h, qev_v)
                for dst, srct in [(dbg_K, K_res), (dbg_V, V_res), (dbg_Q, Q_bf)]:
                    sG.n += 16
                    g.dma_start(out=dst[:, :], in_=srct[:, :]).then_inc(sG.h, 16)
                g.wait_ge(sG.h, sG.n)
            block.gpsimd(_)

        psA.close()
        psA2 = ExitStack()
        ps_S = [psA2.enter_context(nc.psum_tensor(f"ps_S{i}", [128, 2 * QCW], F32))
                for i in range(2)]
        ps_PV = [psA2.enter_context(nc.psum_tensor(f"ps_PV{i}", [128, 512], F32))
                 for i in range(2)]
        ps_misc = psA2.enter_context(nc.psum_tensor("ps_misc", [128, 512], F32))

        last_exp_psS = [0, 0]
        last_pv_Pbf = [sT.n, sT.n]
        evac_done = {-2: sV.n, -1: sV.n}
        negc_ready = {}
        misc_free = sV.n
        last_cmT = [sT.n]
        first_att = True

        for qc in range(NQC):
            for h in range(H):
                idx = qc * H + h
                hb = idx % 2
                cc = h // 2
                rb = 64 * (h % 2)
                need_c = qc == 0
                pv_cnt = {}
                exp_cnt = {}
                s_cnt = {}

                def q_rhs(qc=qc, cc=cc, rb=rb):
                    return Q_bf[rb:rb + E,
                                cc * TOWN + qc * QCW: cc * TOWN + (qc + 1) * QCW]

                def s_lhs(k, cc=cc, rb=rb):
                    return K_res[rb:rb + E,
                                 cc * NKV + 128 * k: cc * NKV + 128 * (k + 1)]

                def pv_step(te, gp, h=h, hb=hb):
                    last = None
                    for j in range(2):
                        k = 2 * gp + j
                        last = te.matmul(
                            ps_PV[hb][0:VW, 0:QCW],
                            V_res[:, k * VROW + VW * h: k * VROW + VW * (h + 1)],
                            P_bf[gp % 2][:, j * QCW:(j + 1) * QCW],
                            start=(k == 0), stop=(k == KCH - 1))
                    pv_cnt[gp] = sT.inc(last)
                    last_pv_Pbf[gp % 2] = pv_cnt[gp]

                for g in range(NG):
                    par = g % 2
                    fa = first_att
                    first_att = False

                    def _(te, g=g, par=par, fa=fa, idx=idx):
                        if fa:
                            te.wait_ge(sGa.h, sL_gather)
                            te.wait_ge(sV.h, qev_v)
                        te.wait_ge(sC.h, last_exp_psS[par])
                        last = None
                        for j in range(2):
                            last = te.matmul(
                                ps_S[par][:, j * QCW:(j + 1) * QCW],
                                s_lhs(2 * g + j), q_rhs(), start=True, stop=True)
                        s_cnt[g] = sT.inc(last)
                        if g >= 1:
                            gp = g - 1
                            te.wait_ge(sC.h, exp_cnt[gp])
                            if gp == 0:
                                te.wait_ge(sV.h, evac_done[idx - 2])
                            pv_step(te, gp)
                    block.tensor(_)

                    if need_c and g == 0:
                        def _(dv, lcm=last_cmT[0]):
                            dv.wait_ge(sT.h, s_cnt[0])
                            dv.wait_ge(sT.h, lcm)
                            dv.reduce_max(cmax[:, 0:1], ps_S[0][:, 0:2 * QCW],
                                          axis=AX.X)
                            dv.engine_nop()
                            dv.engine_nop()
                            sV.inc(dv.tensor_copy(out=cmaxb[:, 0:1],
                                                  in_=cmax[:, 0:1]))
                        block.vector(_)
                        vv = sV.n

                        def _(te, vv=vv, mf=misc_free):
                            te.wait_ge(sV.h, vv)
                            te.wait_ge(sV.h, mf)
                            sT.inc(te.matmul(ps_misc[0:1, 128:256], cmaxb[:, 0:1],
                                             ident_bf[:, :], start=True, stop=True))
                        block.tensor(_)
                        tv = sT.n
                        last_cmT[0] = tv

                        def _(dv, tv=tv):
                            dv.wait_ge(sT.h, tv)
                            dv.reduce_max(cmax[0:1, 1:2], ps_misc[0:1, 128:256],
                                          axis=AX.X, negate=True)
                            dv.engine_nop()
                            sV.inc(dv.engine_nop())
                        block.vector(_)
                        vv2 = sV.n

                        def _(te, vv2=vv2):
                            te.wait_ge(sV.h, vv2)
                            sT.inc(te.matmul(ps_misc[:, 0:1], ones_row[0:1, :],
                                             cmax[0:1, 1:2], start=True, stop=True))
                        block.tensor(_)
                        tv2 = sT.n

                        def _(dv, h=h, tv2=tv2):
                            dv.wait_ge(sT.h, tv2)
                            sV.inc(dv.tensor_copy(out=negC[:, h:h + 1],
                                                  in_=ps_misc[:, 0:1]))
                        block.vector(_)
                        negc_ready[h] = sV.n
                        misc_free = sV.n

                    def _(sc, g=g, par=par, h=h):
                        sc.wait_ge(sT.h, s_cnt[g])
                        if g == 0:
                            sc.wait_ge(sV.h, negc_ready[h])
                        sc.wait_ge(sT.h, last_pv_Pbf[par])
                        exp_cnt[g] = sC.inc(sc.activation(
                            P_bf[par][:, 0:2 * QCW], ps_S[par][:, 0:2 * QCW],
                            AF.Exp, bias=negC[:, h:h + 1]))
                        last_exp_psS[par] = exp_cnt[g]
                    block.scalar(_)

                def _(te, idx=idx):
                    gp = NG - 1
                    te.wait_ge(sC.h, exp_cnt[gp])
                    if gp == 0:
                        te.wait_ge(sV.h, evac_done[idx - 2])
                    pv_step(te, gp)
                block.tensor(_)

                def _(dv, hb=hb):
                    dv.wait_ge(sT.h, pv_cnt[NG - 1])
                    dv.tensor_scalar(zr_t[0:1, 0:QCW], ps_PV[hb][64:65, 0:QCW],
                                     float(math.sqrt(HEAD_DIM)), 1e-30,
                                     OP.mult, OP.add)
                    dv.engine_nop()
                    dv.engine_nop()
                    dv.reciprocal(zr_t[0:1, QCW:2 * QCW], zr_t[0:1, 0:QCW])
                    dv.engine_nop()
                    sV.inc(dv.engine_nop())
                block.vector(_)
                vv3 = sV.n

                def _(te, vv3=vv3, mf=misc_free):
                    te.wait_ge(sV.h, vv3)
                    te.wait_ge(sV.h, mf)
                    sT.inc(te.matmul(ps_misc[0:64, 0:QCW], ones_row[0:1, 0:64],
                                     zr_t[0:1, QCW:2 * QCW], start=True, stop=True))
                block.tensor(_)
                tv3 = sT.n

                def _(dv, tv3=tv3, hb=hb, h=h, cc=cc, rb=rb, qc=qc):
                    dv.wait_ge(sT.h, tv3)
                    dv.tensor_copy(out=zrb_sb[:, 0:QCW], in_=ps_misc[0:64, 0:QCW])
                    dv.engine_nop()
                    dv.engine_nop()
                    sV.inc(dv.tensor_mul(
                        OnT[rb:rb + 64,
                            cc * TOWN + qc * QCW: cc * TOWN + (qc + 1) * QCW],
                        ps_PV[hb][0:64, 0:QCW], zrb_sb[:, 0:QCW]))
                block.vector(_)
                evac_done[idx] = sV.n
                misc_free = sV.n

        att_done_v = sV.n
        if debug_dumps:
            def _(g):
                g.wait_ge(sV.h, att_done_v)
                sG.n += 16
                g.dma_start(out=dbg_OnT[:, :], in_=OnT[:, :]).then_inc(sG.h, 16)
                sG.n += 16
                g.dma_start(out=dbg_negC[:, :], in_=negC[:, :]).then_inc(sG.h, 16)
                g.wait_ge(sG.h, sG.n)
            block.gpsimd(_)
        psA2.close()
        if stop_after == "attn":
            ctxA.close()
            return nc
        ctxA.close()

        ctxB = ExitStack()
        R = sb(ctxB, "R", 128, DCH * TOWN, BF16)
        NMb = [sb(ctxB, f"NMb{i}", 128, QCW, F32) for i in range(NQC)]
        Ib = [sb(ctxB, f"Ib{i}", 128, QCW, F32) for i in range(NQC)]

        ctxB1 = ExitStack()
        WOS = 2
        wo_f = [sb(ctxB1, f"wo_f{i}", 128, 128 * OGW, F32) for i in range(WOS * CCH)]
        wo_b = [sb(ctxB1, f"wo_b{i}", 128, 128 * OGW, BF16) for i in range(WOS * CCH)]
        XRN = 6
        xre = [sb(ctxB1, f"xre{i}", 128, QCW, F32) for i in range(XRN)]
        scrA = [sb(ctxB1, f"scrA{i}", 128, QCW, F32) for i in range(4)]
        y1scr = [sb(ctxB1, f"y1scr{i}", 128, QCW, BF16) for i in range(8)]

        psB1 = ExitStack()
        ps_W = [psB1.enter_context(nc.psum_tensor(f"ps_W{i}", [128, 512], F32))
                for i in range(4)]
        ps_s1 = [psB1.enter_context(nc.psum_tensor(f"ps_s1_{i}", [128, 512], F32))
                 for i in range(NQC)]
        ps_s2 = [psB1.enter_context(nc.psum_tensor(f"ps_s2_{i}", [128, 512], F32))
                 for i in range(NQC)]

        NOG = DCH // OGW
        wo_conv = {}
        wo_mm_done = {}
        scr_sc = {}
        scr_free = {-i: sV.n for i in range(1, 5)}
        y1_free = {-i: sT.n for i in range(1, 9)}
        sq_pw = {}
        SLAG = 3

        def emit_stats(oi2):
            o2_, qc2 = divmod(oi2, NQC)

            def _(te, o=o2_, qc=qc2, oi2=oi2):
                te.wait_ge(sP.h, sq_pw[oi2])
                sl = y1scr[oi2 % 8]
                te.matmul(ps_s1[qc][0:1, 0:QCW], ones_col[:, 0:1],
                          R[:, o * TOWN + qc * QCW: o * TOWN + (qc + 1) * QCW],
                          start=(o == 0), stop=(o == DCH - 1))
                last = te.matmul(ps_s2[qc][0:1, 0:QCW], ones_col[:, 0:1],
                                 sl[:, 0:QCW],
                                 start=(o == 0), stop=(o == DCH - 1))
                y1_free[oi2] = sT.inc(last)
            block.tensor(_)
        xre_free = {-i: sV.n for i in range(1, XRN + 1)}
        oi = 0
        for og in range(NOG):
            def _(q, og=og):
                if og == 0:
                    q.wait_ge(sV.h, att_done_v)
                if og >= WOS:
                    q.wait_ge(sP.h, wo_conv[og - WOS])
                for ccx in range(CCH):
                    sWo[og % WOS].n += 16
                    q.dma_start(
                        out=wo_f[(og % WOS) * CCH + ccx][:, :],
                        in_=woT[128 * ccx:128 * (ccx + 1), 128 * OGW * og:128 * OGW * (og + 1)]
                    ).then_inc(sWo[og % WOS].h, 16)
            block.sync(_)

            def _(dv, og=og):
                dv.wait_ge(sWo[og % WOS].h, 16 * CCH * (og // WOS + 1))
                if og >= WOS:
                    dv.wait_ge(sT.h, wo_mm_done[og - WOS])
                last = None
                for ccx in range(CCH):
                    last = dv.tensor_copy(out=wo_b[(og % WOS) * CCH + ccx][:, :],
                                          in_=wo_f[(og % WOS) * CCH + ccx][:, :])
                wo_conv[og] = sV.inc(last)
            block.vector(_)

            for o4 in range(OGW):
                o = og * OGW + o4
                for qc in range(NQC):
                    par = oi % 4

                    def _(te, og=og, o4=o4, o=o, qc=qc, par=par, oi=oi):
                        te.wait_ge(sV.h, wo_conv[og])
                        if oi >= 4:
                            te.wait_ge(sV.h, scr_sc[oi - 4])
                        last = None
                        for ccx in range(CCH):
                            last = te.matmul(
                                ps_W[par][:, 0:QCW],
                                wo_b[(og % WOS) * CCH + ccx][:, 128 * o4:128 * (o4 + 1)],
                                OnT[:, ccx * TOWN + qc * QCW: ccx * TOWN + (qc + 1) * QCW],
                                start=(ccx == 0), stop=(ccx == CCH - 1))
                        sT.inc(last)
                    block.tensor(_)
                    tw = sT.n

                    def _(q, o=o, qc=qc, oi=oi):
                        q.wait_ge(sV.h, xre_free[oi - XRN])
                        sXre[oi % XRN].n += 16
                        q.dma_start(
                            out=xre[oi % XRN][:, :],
                            in_=x_dT[128 * o:128 * (o + 1), qc * QCW:(qc + 1) * QCW]
                        ).then_inc(sXre[oi % XRN].h, 16)
                    block.sync(_)
                    lw = sXre[oi % XRN].n

                    def _(dv, o=o, qc=qc, par=par, oi=oi, lw=lw, tw=tw):
                        dv.wait_ge(sT.h, tw)
                        if oi >= 4:
                            dv.wait_ge(sV.h, scr_free[oi - 4])
                        dv.tensor_scalar(
                            scrA[oi % 4][:, :], ps_W[par][:, 0:QCW],
                            bo_sb[:, o:o + 1], None, OP.add)
                        dv.wait_ge(sXre[oi % XRN].h, lw)
                        sV.inc(dv.tensor_add(
                            R[:, o * TOWN + qc * QCW: o * TOWN + (qc + 1) * QCW],
                            scrA[oi % 4][:, :], xre[oi % XRN][:, :]))
                    block.vector(_)
                    scr_free[oi] = sV.n
                    scr_sc[oi] = sV.n
                    xre_free[oi] = sV.n
                    vw = sV.n

                    def _(g, o=o, qc=qc, oi=oi, vw=vw):
                        g.wait_ge(sV.h, vw)
                        if oi >= 8:
                            g.wait_ge(sT.h, y1_free[oi - 8])
                        sl = y1scr[oi % 8]
                        sP.inc(g.tensor_mul(
                            sl[:, 0:QCW],
                            R[:, o * TOWN + qc * QCW: o * TOWN + (qc + 1) * QCW],
                            R[:, o * TOWN + qc * QCW: o * TOWN + (qc + 1) * QCW]))
                    block.gpsimd(_)
                    sq_pw[oi] = sP.n

                    if oi >= SLAG:
                        emit_stats(oi - SLAG)
                    oi += 1
            wo_mm_done[og] = sT.n
        for oi2 in range(max(0, oi - SLAG), oi):
            emit_stats(oi2)

        stat_t = sT.n
        for qc in range(NQC):
            prev_sc = sC.n

            def _(dv, qc=qc, prev_sc=prev_sc):
                dv.wait_ge(sT.h, stat_t)
                dv.wait_ge(sC.h, prev_sc)
                nm = negm_r[0:1, qc * QCW:(qc + 1) * QCW]
                dv.tensor_scalar(nm, ps_s1[qc][0:1, 0:QCW], -1.0 / D, None, OP.mult)
                dv.tensor_scalar(lnscr[0:1, 0:QCW], ps_s2[qc][0:1, 0:QCW],
                                 1.0 / D, None, OP.mult)
                dv.engine_nop()
                dv.engine_nop()
                dv.tensor_mul(lnscr[0:1, QCW:2 * QCW], nm, nm)
                sV.inc(dv.scalar_tensor_tensor(lnscr[0:1, 0:QCW], lnscr[0:1, 0:QCW],
                                               1e-5, lnscr[0:1, QCW:2 * QCW],
                                               OP.add, OP.subtract))
            block.vector(_)
            vw = sV.n

            def _(sc, qc=qc, vw=vw):
                sc.wait_ge(sV.h, vw)
                sc.activation(lnscr[0:1, 0:QCW], lnscr[0:1, 0:QCW], AF.Ln)
                sC.inc(sc.activation(inv_r[0:1, qc * QCW:(qc + 1) * QCW],
                                     lnscr[0:1, 0:QCW], AF.Exp, scale=-0.5))
            block.scalar(_)
        ln1_c = sC.n

        if debug_dumps:
            def _(g, c0=ln1_c):
                g.wait_ge(sC.h, c0)
                sG.n += 16
                g.dma_start(out=dbg_R[:, :], in_=R[:, :]).then_inc(sG.h, 16)
                sG.n += 16
                g.dma_start(out=dbg_rows[0:1, 0:NQC * QCW],
                            in_=negm_r[0:1, :]).then_inc(sG.h, 16)
                sG.n += 16
                g.dma_start(out=dbg_rows[0:1, NQC * QCW:2 * NQC * QCW],
                            in_=inv_r[0:1, :]).then_inc(sG.h, 16)
                g.wait_ge(sG.h, sG.n)
            block.gpsimd(_)
        psB1.close()
        ctxB1.close()
        ctxOn.close()

        ctxB2 = ExitStack()
        x1f = sb(ctxB2, "x1f", 128, DCH * TOWN, BF16)
        gw_bf = sb(ctxB2, "gw_bf", 128, NE * DCH, BF16)
        FW = len(sFw)
        fw_f = [sb(ctxB2, f"fw_f{i}", 128, 256, F32) for i in range(FW)]
        fw_b = [sb(ctxB2, f"fw_b{i}", 128, 256, BF16) for i in range(FW)]
        gscr = [sb(ctxB2, f"gscr{i}", 128, QCW, F32) for i in range(2)]
        LW = sb(ctxB2, "LW", 128, 512, F32)
        GA = sb(ctxB2, "GA", 128, 512, F32)
        GB = sb(ctxB2, "GB", 128, 512, F32)
        gu_sb = sb(ctxB2, "gu_sb", NE, QCW, F32)
        wch_sb = sb(ctxB2, "wch_sb", NE, QCW, F32)
        wb_sb = [sb(ctxB2, f"wb_sb{i}", 128, QCW, F32) for i in range(NQC)]
        low_bf = [sb(ctxB2, f"low_bf{i}", 128, QCW, BF16) for i in range(NQC)]
        wb3 = sb(ctxB2, "wb3", 128, NE, BF16)
        tcol = sb(ctxB2, "tcol", 128, 8, F32)
        outst = [gscr[0], gscr[1], LW, GA]
        evsc = [gscr[0], gscr[1], LW, GA]
        NM2b, I2b = NMb[0], Ib[0]

        psB2a = ExitStack()
        ps_m1 = psB2a.enter_context(nc.psum_tensor("ps_m1", [128, 512], F32))
        ps_m2 = psB2a.enter_context(nc.psum_tensor("ps_m2", [128, 512], F32))

        b1_t, b1_v, b1_c, b1_p = sT.n, sV.n, sC.n, sP.n

        fw_uses = [0] * FW

        def _(q):
            q.wait_ge(sT.h, b1_t)
            q.wait_ge(sV.h, b1_v)
            q.wait_ge(sC.h, b1_c)
            q.wait_ge(sP.h, b1_p)
            fw_uses[0] += 1
            sFw[0].n += 16
            q.dma_start(out=fw_f[0][:, 0:NE * DCH],
                        in_=gw_pk[:, :]).then_inc(sFw[0].h, 16)
        block.sync(_)
        lw = sFw[0].n

        def _(g, lw=lw):
            g.wait_ge(sFw[0].h, lw)
            sP.inc(g.tensor_copy(out=gw_bf[:, :], in_=fw_f[0][:, 0:NE * DCH]))
        block.gpsimd(_)
        gw_p = sP.n

        fst = {"j": 0, "conv": {}, "cons": {}, "load": {}}

        def cwait(eng, j):
            pe, pc = fst["conv"][j]
            eng.wait_ge(sV.h if pe == "v" else sP.h, pc)

        def fw_item(src_ap, width, eng="g"):
            j = fst["j"]
            slot = j % FW

            def _(q, j=j, slot=slot, src_ap=src_ap, width=width):
                if j >= FW:
                    cwait(q, j - FW)
                else:
                    q.wait_ge(sP.h, gw_p)
                fw_uses[slot] += 1
                fst["load"][j] = 16 * fw_uses[slot]
                sFw[slot].n += 16
                q.dma_start(out=fw_f[slot][:, 0:width],
                            in_=src_ap).then_inc(sFw[slot].h, 16)
            block.sync(_)

            if eng == "g":
                def _(g, j=j, slot=slot, width=width):
                    g.wait_ge(sFw[slot].h, fst["load"][j])
                    if j >= FW:
                        g.wait_ge(sT.h, fst["cons"][j - FW])
                    fst["conv"][j] = ("g", sP.inc(g.tensor_copy(
                        out=fw_b[slot][:, 0:width], in_=fw_f[slot][:, 0:width])))
                block.gpsimd(_)
            else:
                def _(dv, j=j, slot=slot, width=width):
                    dv.wait_ge(sFw[slot].h, fst["load"][j])
                    if j >= FW:
                        dv.wait_ge(sT.h, fst["cons"][j - FW])
                    fst["conv"][j] = ("v", sV.inc(dv.tensor_copy(
                        out=fw_b[slot][:, 0:width], in_=fw_f[slot][:, 0:width])))
                block.vector(_)
            fst["j"] = j + 1
            return j, slot

        def fst_done(j, cnt):
            fst["cons"][j] = cnt

        bc_v = [None] * NQC
        for qcx in range(NQC):
            def _(te, qcx=qcx):
                te.wait_ge(sC.h, ln1_c)
                if qcx > 0:
                    te.wait_ge(sV.h, bc_v[qcx - 1])
                te.matmul(ps_m1[:, 0:QCW], ones_row[0:1, :],
                          negm_r[0:1, qcx * QCW:(qcx + 1) * QCW],
                          start=True, stop=True)
                sT.inc(te.matmul(ps_m2[:, 0:QCW], ones_row[0:1, :],
                                 inv_r[0:1, qcx * QCW:(qcx + 1) * QCW],
                                 start=True, stop=True))
            block.tensor(_)
            tw = sT.n

            def _(dv, qcx=qcx, tw=tw):
                dv.wait_ge(sT.h, tw)
                dv.tensor_copy(out=NMb[qcx][:, :], in_=ps_m1[:, 0:QCW])
                dv.tensor_copy(out=Ib[qcx][:, :], in_=ps_m2[:, 0:QCW])
                dv.engine_nop()
                bc_v[qcx] = sV.inc(dv.engine_nop())
            block.vector(_)

        x1c = [[None] * DCH for _ in range(NQC)]
        for qcx in range(NQC):
            for dc in range(DCH):
                def _(dv, qcx=qcx, dc=dc):
                    ch = slice(dc * TOWN + qcx * QCW, dc * TOWN + (qcx + 1) * QCW)
                    dv.tensor_add(gscr[dc % 2][:, :], R[:, ch], NMb[qcx][:, :])
                    x1c[qcx][dc] = sV.inc(dv.tensor_mul(
                        x1f[:, ch], gscr[dc % 2][:, :], Ib[qcx][:, :]))
                block.vector(_)
        if debug_dumps:
            def _(g):
                g.wait_ge(sV.h, x1c[NQC - 1][DCH - 1])
                sG.n += 16
                g.dma_start(out=dbg_x1q[:, :], in_=x1f[:, :]).then_inc(sG.h, 16)
                g.wait_ge(sG.h, sG.n)
            block.gpsimd(_)

        NT4 = QCW // 128
        wbv = [None] * NQC
        for qcx in range(NQC):
            def _(te, qcx=qcx):
                te.wait_ge(sP.h, gw_p)
                if qcx > 0:
                    te.wait_ge(sV.h, wbv[qcx - 1])
                last = None
                for dc in range(DCH):
                    te.wait_ge(sV.h, x1c[qcx][dc])
                    last = te.matmul(ps_m1[0:NE, 0:QCW],
                                     gw_bf[:, NE * dc:NE * (dc + 1)],
                                     x1f[:, dc * TOWN + qcx * QCW:
                                         dc * TOWN + (qcx + 1) * QCW],
                                     start=(dc == 0), stop=(dc == DCH - 1))
                sT.inc(last)
            block.tensor(_)
            tw = sT.n

            def _(dv, tw=tw):
                dv.wait_ge(sT.h, tw)
                sV.inc(dv.tensor_copy(out=gu_sb[:, :], in_=ps_m1[0:NE, 0:QCW]))
            block.vector(_)
            vw = sV.n

            def _(te, vw=vw):
                te.wait_ge(sV.h, vw)
                last = None
                for t4 in range(NT4):
                    last = te.matmul(ps_m2[:, 8 * t4:8 * (t4 + 1)],
                                     gu_sb[0:NE, 128 * t4:128 * (t4 + 1)],
                                     ident_f[0:NE, 0:NE], start=True, stop=True)
                sT.inc(last)
            block.tensor(_)
            tw2 = sT.n

            if debug_dumps and qcx == NQC - 1:
                def _(g, tw2=tw2):
                    g.wait_ge(sT.h, tw2)
                    sG.n += 16
                    g.dma_start(out=dbg_gu[:, :], in_=gu_sb[:, :]).then_inc(sG.h, 16)
                    g.wait_ge(sG.h, sG.n)
                block.gpsimd(_)

            wb3_free = [0]
            for t4 in range(NT4):
                def _(dv, t4=t4, tw2=tw2, lwb=wb3_free[0]):
                    dv.wait_ge(sT.h, tw2)
                    dv.wait_ge(sT.h, lwb)
                    dv.tensor_copy(out=LW[:, 0:NE], in_=ps_m2[:, 8 * t4:8 * (t4 + 1)])
                    dv.memset(LW[:, NE:512], -3e38)
                    dv.engine_nop()
                    dv.engine_nop()
                    dv.reduce_max(tcol[:, 0:1], LW[:, :], axis=AX.X)
                    dv.engine_nop()
                    dv.engine_nop()
                    dv.tensor_scalar(GA[:, :], LW[:, :], tcol[:, 0:1], None,
                                     OP.is_equal)
                    dv.scalar_tensor_tensor(GB[:, :], GA[:, :],
                                            -1e30, LW[:, :], OP.mult, OP.add)
                    dv.engine_nop()
                    dv.engine_nop()
                    dv.reduce_max(tcol[:, 2:3], GB[:, :], axis=AX.X)
                    dv.engine_nop()
                    dv.engine_nop()
                    dv.tensor_scalar(GB[:, :], LW[:, :], tcol[:, 2:3],
                                     None, OP.is_ge)
                    dv.tensor_sub(GB[:, :], GB[:, :], GA[:, :])
                    dv.tensor_sub(tcol[:, 3:4], tcol[:, 0:1], tcol[:, 2:3])
                    sV.inc(dv.tensor_sub(tcol[:, 4:5], tcol[:, 2:3],
                                         tcol[:, 0:1]))
                block.vector(_)
                vw2 = sV.n

                def _(sc, vw2=vw2):
                    sc.wait_ge(sV.h, vw2)
                    sc.activation(tcol[:, 5:6], tcol[:, 3:4], AF.Sigmoid)
                    sC.inc(sc.activation(tcol[:, 6:7], tcol[:, 4:5], AF.Sigmoid))
                block.scalar(_)
                cw = sC.n

                def _(dv, cw=cw):
                    dv.wait_ge(sC.h, cw)
                    dv.tensor_scalar(GA[:, :], GA[:, :], tcol[:, 5:6],
                                     None, OP.mult)
                    dv.tensor_scalar(GB[:, :], GB[:, :], tcol[:, 6:7],
                                     None, OP.mult)
                    dv.tensor_add(GA[:, :], GA[:, :], GB[:, :])
                    sV.inc(dv.tensor_copy(out=wb3[:, :], in_=GA[:, 0:NE]))
                block.vector(_)
                vw3 = sV.n

                def _(te, t4=t4, vw3=vw3):
                    te.wait_ge(sV.h, vw3)
                    sT.inc(te.matmul(ps_m1[0:NE, 128 * t4:128 * (t4 + 1)],
                                     wb3[:, :], ident_bf[:, :],
                                     start=True, stop=True))
                block.tensor(_)
                wb3_free[0] = sT.n
                if debug_dumps and qcx == NQC - 1 and t4 == NT4 - 1:
                    def _(g, vw3=vw3):
                        g.wait_ge(sV.h, vw3)
                        sG.n += 16
                        g.dma_start(out=dbg_tc[:, :], in_=tcol[:, :]).then_inc(sG.h, 16)
                        g.wait_ge(sG.h, sG.n)
                    block.gpsimd(_)
            tw3 = sT.n

            def _(dv, tw3=tw3):
                dv.wait_ge(sT.h, tw3)
                sV.inc(dv.tensor_copy(out=wch_sb[:, :], in_=ps_m1[0:NE, 0:QCW]))
            block.vector(_)
            vw4 = sV.n

            def _(te, vw4=vw4):
                te.wait_ge(sV.h, vw4)
                sT.inc(te.matmul(ps_m2[:, 0:QCW], sel_f[:, :], wch_sb[:, :],
                                 start=True, stop=True))
            block.tensor(_)
            tw4 = sT.n

            def _(dv, tw4=tw4, qcx=qcx):
                dv.wait_ge(sT.h, tw4)
                wbv[qcx] = sV.inc(dv.tensor_copy(out=wb_sb[qcx][:, :],
                                                 in_=ps_m2[:, 0:QCW]))
            block.vector(_)
            if debug_dumps and qcx == NQC - 1:
                def _(g, qcx=qcx):
                    g.wait_ge(sV.h, wbv[qcx])
                    sG.n += 16
                    g.dma_start(out=dbg_wch[:, :], in_=wch_sb[:, :]).then_inc(sG.h, 16)
                    g.wait_ge(sG.h, sG.n)
                block.gpsimd(_)

        for dc in range(DCH):
            j, slot = fw_item(la_pk[:, 128 * dc:128 * (dc + 1)], 128)

            def _(te, dc=dc, j=j, slot=slot):
                cwait(te, j)
                if dc == 0:
                    te.wait_ge(sV.h, wbv[NQC - 1])
                last = None
                for qcx in range(NQC):
                    pb = ps_m1 if qcx == 0 else ps_m2
                    last = te.matmul(pb[:, 0:QCW], fw_b[slot][:, 0:128],
                                     x1f[:, dc * TOWN + qcx * QCW:
                                         dc * TOWN + (qcx + 1) * QCW],
                                     start=(dc == 0), stop=(dc == DCH - 1))
                fst_done(j, sT.inc(last))
            block.tensor(_)
        tw5 = sT.n

        def _(dv, tw5=tw5):
            dv.wait_ge(sT.h, tw5)
            for qcx in range(NQC):
                pb = ps_m1 if qcx == 0 else ps_m2
                sV.inc(dv.tensor_mul(low_bf[qcx][:, :], pb[:, 0:QCW],
                                     wb_sb[qcx][:, :]))
        block.vector(_)
        low_v = sV.n
        if debug_dumps:
            def _(g):
                g.wait_ge(sV.h, low_v)
                sG.n += 16
                g.dma_start(out=dbg_low[:, :], in_=low_bf[0][:, :]).then_inc(sG.h, 16)
                g.wait_ge(sG.h, sG.n)
            block.gpsimd(_)

        psB2a.close()
        psB2b = ExitStack()
        NB = 2 * NQC
        ps_B = [psB2b.enter_context(nc.psum_tensor(f"ps_B{i}", [128, 512], F32))
                for i in range(2 * NB)]

        bw_done = {}
        ev_v = {-2: sV.n, -1: sV.n}
        NOB = DCH // 2

        def emit_evac(ob):
            def _(dv, ob=ob):
                dv.wait_ge(sT.h, bw_done[ob])
                for o2 in range(2):
                    for qcx in range(NQC):
                        ei = o2 * NQC + qcx
                        dv.tensor_scalar(evsc[ei][:, 0:QCW],
                                         ps_B[(ob % 2) * NB + ei][:, 0:QCW],
                                         bb_sb[:, ob * 2 + o2:ob * 2 + o2 + 1],
                                         None, OP.add)
                for o2 in range(2):
                    for qcx in range(NQC):
                        ei = o2 * NQC + qcx
                        o = ob * 2 + o2
                        ch = slice(o * TOWN + qcx * QCW,
                                   o * TOWN + (qcx + 1) * QCW)
                        sV.inc(dv.tensor_add(R[:, ch], x1f[:, ch],
                                             evsc[ei][:, 0:QCW]))
            block.vector(_)
            ev_v[ob] = sV.n

        for ob in range(NOB):
            for dc in range(DCH):
                j, slot = fw_item(
                    bwT[128 * dc:128 * (dc + 1), 256 * ob:256 * (ob + 1)], 256,
                    eng="v" if dc % 2 else "g")

                def _(te, j=j, slot=slot, ob=ob, dc=dc):
                    cwait(te, j)
                    if dc == 0:
                        te.wait_ge(sV.h, low_v)
                        te.wait_ge(sV.h, ev_v[ob - 2])
                    last = None
                    for o2 in range(2):
                        for qcx in range(NQC):
                            last = te.matmul(
                                ps_B[(ob % 2) * NB + o2 * NQC + qcx][:, 0:QCW],
                                fw_b[slot][:, 128 * o2:128 * (o2 + 1)],
                                x1f[:, dc * TOWN + qcx * QCW:
                                    dc * TOWN + (qcx + 1) * QCW],
                                start=(dc == 0), stop=False)
                    fst_done(j, sT.inc(last))
                block.tensor(_)

            j, slot = fw_item(lb_pk[0:ER, 256 * ob:256 * (ob + 1)], 256)

            def _(te, j=j, slot=slot, ob=ob):
                cwait(te, j)
                last = None
                for o2 in range(2):
                    for qcx in range(NQC):
                        last = te.matmul(
                            ps_B[(ob % 2) * NB + o2 * NQC + qcx][:, 0:QCW],
                            fw_b[slot][0:ER, 128 * o2:128 * (o2 + 1)],
                            low_bf[qcx][:, :], start=False, stop=True)
                fst_done(j, sT.inc(last))
                bw_done[ob] = sT.n
            block.tensor(_)

            if ob >= 1:
                emit_evac(ob - 1)
        emit_evac(NOB - 1)

        psB2b.close()
        psB2c = ExitStack()
        ps_st = [psB2c.enter_context(nc.psum_tensor(f"ps_st{i}", [128, 512], F32))
                 for i in range(2 * NQC)]
        ps_n1 = psB2c.enter_context(nc.psum_tensor("ps_n1", [128, 512], F32))
        ps_n2 = psB2c.enter_context(nc.psum_tensor("ps_n2", [128, 512], F32))

        sq_v = {}

        def _(dv):
            dv.wait_ge(sT.h, bw_done[NOB - 1])
            for dc in range(DCH):
                for qcx in range(NQC):
                    ch = slice(dc * TOWN + qcx * QCW, dc * TOWN + (qcx + 1) * QCW)
                    sq_v[(dc, qcx)] = sV.inc(dv.tensor_mul(
                        x1f[:, ch], R[:, ch], R[:, ch]))
        block.vector(_)

        st_done = [None] * NQC

        def _(te):
            te.wait_ge(sV.h, ev_v[NOB - 1])
            for dc in range(DCH):
                for qcx in range(NQC):
                    ch = slice(dc * TOWN + qcx * QCW, dc * TOWN + (qcx + 1) * QCW)
                    te.wait_ge(sV.h, sq_v[(dc, qcx)])
                    te.matmul(ps_st[2 * qcx][0:1, 0:QCW], ones_col[:, 0:1],
                              R[:, ch], start=(dc == 0), stop=(dc == DCH - 1))
                    last = te.matmul(ps_st[2 * qcx + 1][0:1, 0:QCW],
                                     ones_col[:, 0:1], x1f[:, ch],
                                     start=(dc == 0), stop=(dc == DCH - 1))
                    if dc == DCH - 1:
                        st_done[qcx] = sT.inc(last)
        block.tensor(_)

        SA = lnscr[0:1, 0:QCW]
        SB = lnscr[0:1, QCW:2 * QCW]
        SC2 = lnscr[0:1, 2 * QCW:3 * QCW]
        bcT = [None] * NQC
        out_cnt = 0
        prev_c2 = sC.n
        apv_prev = sV.n
        app_prev = sP.n
        for qcx in range(NQC):
            def _(dv, qcx=qcx, pc=prev_c2):
                dv.wait_ge(sT.h, st_done[qcx])
                dv.wait_ge(sC.h, pc)
                if qcx > 0:
                    dv.wait_ge(sT.h, bcT[qcx - 1])
                dv.tensor_scalar(SA, ps_st[2 * qcx + 1][0:1, 0:QCW], 1.0 / D,
                                 None, OP.mult)
                dv.tensor_scalar(SC2, ps_st[2 * qcx][0:1, 0:QCW], -1.0 / D,
                                 None, OP.mult)
                dv.engine_nop()
                dv.engine_nop()
                dv.tensor_mul(SB, SC2, SC2)
                sV.inc(dv.scalar_tensor_tensor(SA, SA, 1e-5, SB,
                                               OP.add, OP.subtract))
            block.vector(_)
            vw6 = sV.n

            def _(sc, vw6=vw6):
                sc.wait_ge(sV.h, vw6)
                sc.activation(SA, SA, AF.Ln)
                sC.inc(sc.activation(SB, SA, AF.Exp, scale=-0.5))
            block.scalar(_)
            cw2 = sC.n
            prev_c2 = cw2

            def _(te, qcx=qcx, cw2=cw2, apv=apv_prev, app=app_prev):
                te.wait_ge(sC.h, cw2)
                if qcx > 0:
                    te.wait_ge(sV.h, apv)
                    te.wait_ge(sP.h, app)
                te.matmul(ps_n1[:, 0:QCW], ones_row[0:1, :], SC2,
                          start=True, stop=True)
                bcT[qcx] = sT.inc(te.matmul(ps_n2[:, 0:QCW], ones_row[0:1, :],
                                            SB, start=True, stop=True))
            block.tensor(_)
            tw6 = sT.n

            def _(dv, tw6=tw6):
                dv.wait_ge(sT.h, tw6)
                dv.tensor_copy(out=NM2b[:, :], in_=ps_n1[:, 0:QCW])
                dv.tensor_copy(out=I2b[:, :], in_=ps_n2[:, 0:QCW])
                dv.engine_nop()
                sV.inc(dv.engine_nop())
            block.vector(_)
            bc2_v = sV.n

            for dc in range(DCH):
                ci = out_cnt + dc
                slot = ci % 4
                u = ci // 4
                ch = slice(dc * TOWN + qcx * QCW, dc * TOWN + (qcx + 1) * QCW)
                if ci % 4 == 3:
                    def _(g, slot=slot, u=u, ch=ch, bc2_v=bc2_v):
                        g.wait_ge(sV.h, bc2_v)
                        if u >= 1:
                            g.wait_ge(sOu[slot].h, 16 * u)
                        g.tensor_add(outst[slot][:, :], R[:, ch], NM2b[:, :])
                        sP.inc(g.tensor_mul(outst[slot][:, :],
                                            outst[slot][:, :], I2b[:, :]))
                    block.gpsimd(_)
                    prod = (sP.h, sP.n)
                    app_prev = sP.n
                else:
                    def _(dv, slot=slot, u=u, ch=ch):
                        if u >= 1:
                            dv.wait_ge(sOu[slot].h, 16 * u)
                        dv.tensor_add(outst[slot][:, :], R[:, ch], NM2b[:, :])
                        sV.inc(dv.tensor_mul(outst[slot][:, :],
                                             outst[slot][:, :], I2b[:, :]))
                    block.vector(_)
                    prod = (sV.h, sV.n)
                    apv_prev = sV.n

                def _(q, dc=dc, qcx=qcx, slot=slot, prod=prod):
                    q.wait_ge(prod[0], prod[1])
                    sOu[slot].n += 16
                    q.dma_start(out=out_dT[128 * dc:128 * (dc + 1),
                                           qcx * QCW:(qcx + 1) * QCW],
                                in_=outst[slot][:, :]).then_inc(sOu[slot].h, 16)
                block.sync(_)
            out_cnt += DCH

        psB2c.close()
        ctxB2.close()
        ctxB.close()

        def _(g):
            for s in sOu:
                if s.n:
                    g.wait_ge(s.h, s.n)
            if debug_dumps:
                sG.n += 16
                g.dma_start(out=dbg_y2[:, :], in_=R[:, :]).then_inc(sG.h, 16)
                g.wait_ge(sG.h, sG.n)
        block.gpsimd(_)
    return nc


def host_layout(inputs, D, NKV, TOWN, H):
    E = 64
    HE = H * E
    CCH = HE // 128
    DCH = D // 128
    x = np.asarray(inputs["L0_attn_x"], dtype=np.float32)
    wqk = np.asarray(inputs["L0_attn_Wqk_w"], dtype=np.float32)
    bqk = np.asarray(inputs["L0_attn_Wqk_b"], dtype=np.float32)
    wv = np.asarray(inputs["L0_attn_Wv_w"], dtype=np.float32)
    bv = np.asarray(inputs["L0_attn_Wv_b"], dtype=np.float32)
    wo = np.asarray(inputs["L0_attn_Wo_w"], dtype=np.float32)
    bo = np.asarray(inputs["L0_attn_Wo_b"], dtype=np.float32)
    gw = np.asarray(inputs["L0_ffn_gate_w"], dtype=np.float32)
    bw = np.asarray(inputs["L0_ffn_base_w"], dtype=np.float32)
    bb = np.asarray(inputs["L0_ffn_base_b"], dtype=np.float32)
    la = np.asarray(inputs["L0_ffn_lora_A"], dtype=np.float32)
    lb = np.asarray(inputs["L0_ffn_lora_B"], dtype=np.float32)

    C = np.ascontiguousarray
    shared = dict(
        wqT=C(wqk.T[:, 0::2]),
        wkT=C(wqk.T[:, 1::2]),
        bq_r=C(bqk[0::2].reshape(CCH, 128)),
        bk_r=C(bqk[1::2].reshape(CCH, 128)),
        wvT=C(wv.T),
        bv_r=C(bv.reshape(1, HE)),
        woT=C(wo.T),
        bo_r=C(bo.reshape(DCH, 128)),
        gw_pk=C(gw.T.reshape(DCH, 128, NE).transpose(1, 0, 2).reshape(128, NE * DCH)),
        bwT=C(bw.T),
        bb_r=C(bb.reshape(DCH, 128)),
        la_pk=C(la.reshape(ER, D).T.reshape(DCH, 128, ER).transpose(1, 0, 2)
                .reshape(128, ER * DCH)),
        lb_pk=C(lb.transpose(0, 2, 1).reshape(ER, D)),
    )
    in_maps = []
    for c in range(8):
        b = c // 4
        s = c % 4
        m = dict(shared)
        m["x_dT"] = C(x[b, TOWN * s:TOWN * (s + 1), :].T)
        in_maps.append(m)
    return in_maps


def kernel(**inputs):
    D, NKV, TOWN, H = 4096, 4096, 1024, 8
    nc = build_program(D, NKV, TOWN, H)
    in_maps = host_layout(inputs, D, NKV, TOWN, H)
    res = run_bass_kernel_spmd(nc, in_maps, core_ids=list(range(8)))
    out = np.empty((2, NKV, D), dtype=np.float32)
    for c in range(8):
        b, s = c // 4, c % 4
        out[b, TOWN * s:TOWN * (s + 1), :] = res.results[c]["out_dT"].T
    return out


# version 17
# speedup vs baseline: 1.0038x; 1.0038x over previous
import math
from contextlib import ExitStack

import numpy as np
import ml_dtypes
import concourse.bass as bass
import concourse.mybir as mybir
from concourse.bass_utils import run_bass_kernel_spmd

F32 = mybir.dt.float32
BF16 = mybir.dt.bfloat16
AF = mybir.ActivationFunctionType
OP = mybir.AluOpType
AX = mybir.AxisListType
bfnp = ml_dtypes.bfloat16

NE = 8
RL = 16
ER = NE * RL
LORA_SCALING = 2.0


class Sem:
    def __init__(self, nc, ctx, name):
        self.h = ctx.enter_context(nc.semaphore(name))
        self.n = 0

    def inc(self, instr, v=1):
        instr.then_inc(self.h, v)
        self.n += v
        return self.n


def build_program(D, NKV, TOWN, H, detect_races=True, stop_after=None,
                  debug_dumps=False):
    E = 64
    HE = H * E
    HEAD_DIM = D // H
    CCH = HE // 128
    DCH = D // 128
    KCH = NKV // 128
    TT = TOWN // 128
    NQC = max(1, TOWN // 512)
    QCW = TOWN // NQC
    VW = E + 1
    VROW = H * VW
    NG = KCH // 2
    GRP = 4
    assert HE % 128 == 0 and D % 128 == 0 and NKV % 128 == 0 and TOWN % 128 == 0
    assert KCH % 2 == 0 and QCW <= 512 and HE <= 512
    OGW = 4 if DCH % 4 == 0 else DCH
    assert DCH % OGW == 0 and DCH % 2 == 0

    nc = bass.Bass("TRN2", target_bir_lowering=False,
                   detect_race_conditions=detect_races)

    x_dT = nc.dram_tensor("x_dT", [D, TOWN], F32, kind="ExternalInput")
    wqT = nc.dram_tensor("wqT", [D, HE], F32, kind="ExternalInput")
    wkT = nc.dram_tensor("wkT", [D, HE], F32, kind="ExternalInput")
    wvT = nc.dram_tensor("wvT", [D, HE], F32, kind="ExternalInput")
    bq_r = nc.dram_tensor("bq_r", [CCH, 128], F32, kind="ExternalInput")
    bk_r = nc.dram_tensor("bk_r", [CCH, 128], F32, kind="ExternalInput")
    bv_r = nc.dram_tensor("bv_r", [1, HE], F32, kind="ExternalInput")
    woT = nc.dram_tensor("woT", [HE, D], F32, kind="ExternalInput")
    bo_r = nc.dram_tensor("bo_r", [DCH, 128], F32, kind="ExternalInput")
    gw_pk = nc.dram_tensor("gw_pk", [128, NE * DCH], F32, kind="ExternalInput")
    bwT = nc.dram_tensor("bwT", [D, D], F32, kind="ExternalInput")
    bb_r = nc.dram_tensor("bb_r", [DCH, 128], F32, kind="ExternalInput")
    la_pk = nc.dram_tensor("la_pk", [128, ER * DCH], F32, kind="ExternalInput")
    lb_pk = nc.dram_tensor("lb_pk", [ER, D], F32, kind="ExternalInput")
    out_dT = nc.dram_tensor("out_dT", [D, TOWN], F32, kind="ExternalOutput")
    if debug_dumps:
        dbg_OnT = nc.dram_tensor("dbg_OnT", [128, CCH * TOWN], BF16, kind="ExternalOutput")
        dbg_negC = nc.dram_tensor("dbg_negC", [128, H], F32, kind="ExternalOutput")
        dbg_rows = nc.dram_tensor("dbg_rows", [1, 2 * NQC * QCW], F32, kind="ExternalOutput")
        dbg_R = nc.dram_tensor("dbg_R", [128, DCH * TOWN], BF16, kind="ExternalOutput")
        dbg_x1q = nc.dram_tensor("dbg_x1q", [128, DCH * TOWN], BF16, kind="ExternalOutput")
        dbg_wch = nc.dram_tensor("dbg_wch", [NE, QCW], F32, kind="ExternalOutput")
        dbg_low = nc.dram_tensor("dbg_low", [128, QCW], BF16, kind="ExternalOutput")
        dbg_K = nc.dram_tensor("dbg_K", [128, CCH * NKV], BF16, kind="ExternalOutput")
        dbg_gu = nc.dram_tensor("dbg_gu", [NE, QCW], F32, kind="ExternalOutput")
        dbg_glt = nc.dram_tensor("dbg_glt", [128, 8 * 4], F32, kind="ExternalOutput")
        dbg_y2 = nc.dram_tensor("dbg_y2", [128, DCH * TOWN], BF16, kind="ExternalOutput")
        dbg_tc = nc.dram_tensor("dbg_tc", [128, 8], F32, kind="ExternalOutput")
        dbg_x1q0 = nc.dram_tensor("dbg_x1q0", [128, DCH * QCW], BF16, kind="ExternalOutput")
        dbg_V = nc.dram_tensor("dbg_V", [128, KCH * VROW], BF16, kind="ExternalOutput")
        dbg_Q = nc.dram_tensor("dbg_Q", [128, CCH * TOWN], BF16, kind="ExternalOutput")

    ccK_in = nc.dram_tensor("ccK_in", [HE, TOWN], BF16, kind="Internal")
    ccK_out = nc.dram_tensor("ccK_out", [GRP * HE, TOWN], BF16, kind="Internal")
    ccV_in = nc.dram_tensor("ccV_in", [TOWN, VROW], BF16, kind="Internal")
    ccV_out = nc.dram_tensor("ccV_out", [GRP * TOWN, VROW], BF16, kind="Internal")

    ident_bf_d = nc.inline_tensor(np.eye(128, dtype=bfnp), "c_ident_bf")
    ident_f_d = nc.inline_tensor(np.eye(128, dtype=np.float32), "c_ident_f")
    ones_row_d = nc.inline_tensor(np.ones((1, 128), dtype=np.float32), "c_ones_row")
    ones_col_d = nc.inline_tensor(np.ones((128, 1), dtype=bfnp), "c_ones_col")
    sel_d = nc.inline_tensor(
        LORA_SCALING * np.kron(np.eye(NE, dtype=np.float32),
                               np.ones((1, RL), dtype=np.float32)),
        "c_sel_f")

    est = ExitStack()
    with est:
        def sb(ctx, name, p, f, dtype):
            return ctx.enter_context(nc.sbuf_tensor(name, [p, f], dtype))

        ident_bf = sb(est, "ident_bf_s", 128, 128, BF16)
        ident_f = sb(est, "ident_f_s", 128, 128, F32)
        ones_row = sb(est, "ones_row_s", 1, 128, F32)
        ones_col = sb(est, "ones_col_s", 128, 1, BF16)
        sel_f = sb(est, "sel_s", NE, 128, F32)
        bq_sb = sb(est, "bq_sb", 128, CCH, F32)
        bk_sb = sb(est, "bk_sb", 128, CCH, F32)
        bo_sb = sb(est, "bo_sb", 128, DCH, F32)
        bb_sb = sb(est, "bb_sb", 128, DCH, F32)
        negC = sb(est, "negC", 128, H, F32)
        negm_r = sb(est, "negm_r", 1, NQC * QCW, F32)
        inv_r = sb(est, "inv_r", 1, NQC * QCW, F32)
        lnscr = sb(est, "lnscr", 1, 3 * QCW, F32)
        ctxOn = ExitStack()
        OnT = ctxOn.enter_context(
            nc.sbuf_tensor("OnT", [128, CCH * TOWN], BF16, side="right"))
        vb_sb = ctxOn.enter_context(
            nc.sbuf_tensor("vb_sb", [128, HE], F32, side="right"))

        sL = Sem(nc, est, "sL")
        sB = Sem(nc, est, "sB")
        sXr = [Sem(nc, est, f"sXr{i}") for i in range(3)]
        sWr = [Sem(nc, est, f"sWr{i}") for i in range(3)]
        sGa = Sem(nc, est, "sGa")
        sXre = [Sem(nc, est, f"sXre{i}") for i in range(6)]
        sWo = [Sem(nc, est, f"sWo{i}") for i in range(2)]
        sFw = [Sem(nc, est, f"sFw{i}") for i in range(8)]
        sOu = [Sem(nc, est, f"sOu{i}") for i in range(8)]
        sG = Sem(nc, est, "sG")
        sO = Sem(nc, est, "sO")
        sCK = Sem(nc, est, "sCK")
        sCV = Sem(nc, est, "sCV")
        sT = Sem(nc, est, "sT")
        sV = Sem(nc, est, "sV")
        sC = Sem(nc, est, "sC")
        sP = Sem(nc, est, "sP")

        block = est.enter_context(nc.Block())

        def _(q):
            for dst, src in [(ident_bf, ident_bf_d), (ident_f, ident_f_d),
                             (ones_row, ones_row_d), (ones_col, ones_col_d),
                             (sel_f, sel_d)]:
                sB.n += 16
                q.dma_start(out=dst[:, :], in_=src[:, :]).then_inc(sB.h, 16)
        block.sync(_)
        n_const = sB.n

        ps0 = ExitStack()
        ps_bias = ps0.enter_context(nc.psum_tensor("ps_bias", [128, 512], F32))
        braw = ps0.enter_context(nc.sbuf_tensor("bias_raw", [max(DCH, CCH), 128], F32))
        bvraw = ps0.enter_context(nc.sbuf_tensor("bv_raw", [1, HE], F32))

        for i, (src, nch, dst) in enumerate(
                [(bq_r, CCH, bq_sb), (bk_r, CCH, bk_sb),
                 (bo_r, DCH, bo_sb), (bb_r, DCH, bb_sb)]):
            vprev = sV.n

            def _(q, src=src, nch=nch, vprev=vprev, i=i):
                if i > 0:
                    q.wait_ge(sV.h, vprev)
                sB.n += 16
                q.dma_start(out=braw[0:nch, :], in_=src[:, :]).then_inc(sB.h, 16)
            block.sync(_)
            lw = sB.n

            def _(te, nch=nch, lw=lw, vprev=vprev, i=i):
                te.wait_ge(sB.h, lw)
                if i != 0:
                    te.wait_ge(sV.h, vprev)
                sT.inc(te.transpose(ps_bias[0:128, 0:nch], braw[0:nch, 0:128],
                                    ident_f[0:nch, 0:nch]))
            block.tensor(_)
            tw = sT.n

            def _(dv, dst=dst, nch=nch, tw=tw):
                dv.wait_ge(sT.h, tw)
                sV.inc(dv.tensor_copy(out=dst[:, 0:nch], in_=ps_bias[:, 0:nch]))
            block.vector(_)

        vprev = sV.n

        def _(q):
            sB.n += 16
            q.dma_start(out=bvraw[:, :], in_=bv_r[:, :]).then_inc(sB.h, 16)
        block.sync(_)
        lw = sB.n

        def _(te, lw=lw, vprev=vprev):
            te.wait_ge(sB.h, lw)
            te.wait_ge(sV.h, vprev)
            sT.inc(te.matmul(ps_bias[:, 0:HE], ones_row[0:1, :], bvraw[0:1, :],
                             start=True, stop=True))
        block.tensor(_)
        tw = sT.n

        def _(dv, tw=tw):
            dv.wait_ge(sT.h, tw)
            sV.inc(dv.tensor_copy(out=vb_sb[:], in_=ps_bias[:, 0:HE]))
        block.vector(_)
        bias_ready_v = sV.n
        ps0.close()

        ctxA = ExitStack()
        xbf = sb(ctxA, "xbf", 128, DCH * TOWN, BF16)
        K_res = sb(ctxA, "K_res", 128, CCH * NKV, BF16)
        V_res = sb(ctxA, "V_res", 128, KCH * VROW, BF16)
        Q_bf = sb(ctxA, "Q_bf", 128, CCH * TOWN, BF16)
        KV_own = sb(ctxA, "KV_own", 128, max(CCH * TOWN, TT * VROW), BF16)
        XR = 3
        xin = [sb(ctxA, f"xin{i}", 128, TOWN, F32) for i in range(XR)]
        WR = 3
        w_f = [sb(ctxA, f"w_f{i}", 128, HE, F32) for i in range(WR)]
        w_b = [sb(ctxA, f"w_b{i}", 128, HE, BF16) for i in range(WR)]
        P_bf = [sb(ctxA, f"P_bf{i}", 128, 2 * QCW, BF16) for i in range(2)]
        zr_t = sb(ctxA, "zr_t", 1, 2 * QCW, F32)
        zrb_sb = sb(ctxA, "zrb_sb", 64, QCW, F32)
        cmax = sb(ctxA, "cmax", 128, 2, F32)
        cmaxb = sb(ctxA, "cmaxb", 128, 1, BF16)

        psA = ExitStack()
        NPROJ = max(CCH * NQC, TT)
        ps_proj = psA.enter_context(nc.psum_tensor("ps_proj", [128, 512 * NPROJ], F32))

        sP_x0 = sP.n

        def _(q):
            for dc in range(DCH):
                if dc >= XR:
                    q.wait_ge(sP.h, sP_x0 + dc - XR + 1)
                sXr[dc % XR].n += 16
                q.dma_start(out=xin[dc % XR][:, :],
                            in_=x_dT[128 * dc:128 * (dc + 1), :]
                            ).then_inc(sXr[dc % XR].h, 16)
        block.sync(_)

        def _(g):
            for dc in range(DCH):
                g.wait_ge(sXr[dc % XR].h, 16 * (dc // XR + 1))
                sP.inc(g.tensor_copy(out=xbf[:, dc * TOWN:(dc + 1) * TOWN],
                                     in_=xin[dc % XR][:, :]))
        block.gpsimd(_)
        assert sP.n == sP_x0 + DCH
        x_conv = [sP_x0 + dc + 1 for dc in range(DCH)]

        wr_uses = [0, 0, 0]

        def proj_pass(w_dram, ngrp, kind, free_sem, free_cnt, prev_conv_final):
            sWc0 = sP.n
            sT0 = sT.n
            wl = []

            def _(q):
                for dc in range(DCH):
                    if dc >= WR:
                        q.wait_ge(sP.h, sWc0 + dc - WR + 1)
                    elif prev_conv_final is not None:
                        q.wait_ge(sP.h, prev_conv_final)
                    s = dc % WR
                    wr_uses[s] += 1
                    wl.append(16 * wr_uses[s])
                    sWr[s].n += 16
                    q.dma_start(out=w_f[s][:, :],
                                in_=w_dram[128 * dc:128 * (dc + 1), :]
                                ).then_inc(sWr[s].h, 16)
            block.sync(_)

            def _(g):
                for dc in range(DCH):
                    g.wait_ge(sWr[dc % WR].h, wl[dc])
                    if dc >= WR:
                        g.wait_ge(sT.h, sT0 + (dc - WR + 1))
                    sP.inc(g.tensor_copy(out=w_b[dc % WR][:, :], in_=w_f[dc % WR][:, :]))
            block.gpsimd(_)
            assert sP.n == sWc0 + DCH

            grp_cnt = []

            def _(te):
                te.wait_ge(free_sem.h, free_cnt)
                for dc in range(DCH):
                    te.wait_ge(sP.h, sWc0 + dc + 1)
                    te.wait_ge(sP.h, x_conv[dc])
                    last = None
                    for a in range(ngrp):
                        if kind == "ck":
                            cc, qc = divmod(a, NQC)
                            last = te.matmul(
                                ps_proj[:, 512 * a:512 * a + QCW],
                                w_b[dc % WR][:, 128 * cc:128 * (cc + 1)],
                                xbf[:, dc * TOWN + qc * QCW: dc * TOWN + (qc + 1) * QCW],
                                start=(dc == 0), stop=(dc == DCH - 1))
                        else:
                            last = te.matmul(
                                ps_proj[:, 512 * a:512 * a + HE],
                                xbf[:, dc * TOWN + 128 * a: dc * TOWN + 128 * (a + 1)],
                                w_b[dc % WR][:, 0:HE],
                                start=(dc == 0), stop=(dc == DCH - 1))
                        if dc == DCH - 1:
                            grp_cnt.append(sT.inc(last))
                    if dc < DCH - 1:
                        sT.inc(last)
            block.tensor(_)
            assert sT.n == sT0 + (DCH - 1) + ngrp
            return grp_cnt

        kg = proj_pass(wkT, CCH * NQC, "ck", sV, bias_ready_v, None)
        k_conv_final = sP.n

        def _(dv):
            dv.wait_ge(sV.h, bias_ready_v)
            for a in range(CCH * NQC):
                cc, qc = divmod(a, NQC)
                dv.wait_ge(sT.h, kg[a])
                sV.inc(dv.tensor_scalar(
                    KV_own[:, cc * TOWN + qc * QCW: cc * TOWN + (qc + 1) * QCW],
                    ps_proj[:, 512 * a:512 * a + QCW],
                    bk_sb[:, cc:cc + 1], None, OP.add))
        block.vector(_)
        kev_v = sV.n

        def _(g):
            g.wait_ge(sV.h, kev_v)
            for cc in range(CCH):
                sG.n += 16
                g.dma_start(out=ccK_in[128 * cc:128 * (cc + 1), :],
                            in_=KV_own[:, cc * TOWN:(cc + 1) * TOWN]).then_inc(sG.h, 16)
            g.wait_ge(sG.h, sG.n)
            sCK.inc(g.collective_compute(
                "AllGather", OP.bypass,
                replica_groups=[[0, 1, 2, 3], [4, 5, 6, 7]],
                ins=[ccK_in.ap().opt()], outs=[ccK_out.ap().opt()]))
        block.gpsimd(_)

        vg = proj_pass(wvT, TT, "v", sV, kev_v, k_conv_final)
        v_conv_final = sP.n

        ccK_dma_g = sG.n

        def _(dv):
            dv.wait_ge(sV.h, bias_ready_v)
            dv.wait_ge(sG.h, ccK_dma_g)
            for tt in range(TT):
                dv.wait_ge(sT.h, vg[tt])
                last = None
                for h in range(H):
                    last = dv.tensor_add(
                        KV_own[:, tt * VROW + VW * h: tt * VROW + VW * h + E],
                        ps_proj[:, 512 * tt + E * h: 512 * tt + E * (h + 1)],
                        vb_sb[:, E * h:E * (h + 1)])
                sV.inc(last)
        block.vector(_)
        sV_V = sV.n

        def _(g):
            g.wait_ge(sV.h, sV_V)
            last = None
            for tt in range(TT):
                for h in range(H):
                    last = g.memset(
                        KV_own[:, tt * VROW + VW * h + E: tt * VROW + VW * (h + 1)], 1.0)
            sP.inc(last)
            for tt in range(TT):
                sG.n += 16
                g.dma_start(out=ccV_in[128 * tt:128 * (tt + 1), :],
                            in_=KV_own[:, tt * VROW:(tt + 1) * VROW]).then_inc(sG.h, 16)
            g.wait_ge(sG.h, sG.n)
            sCV.inc(g.collective_compute(
                "AllGather", OP.bypass,
                replica_groups=[[0, 1, 2, 3], [4, 5, 6, 7]],
                ins=[ccV_in.ap().opt()], outs=[ccV_out.ap().opt()]))
        block.gpsimd(_)

        qg = proj_pass(wqT, CCH * NQC, "ck", sV, sV_V, v_conv_final)

        def _(dv):
            for a in range(CCH * NQC):
                cc, qc = divmod(a, NQC)
                dv.wait_ge(sT.h, qg[a])
                sV.inc(dv.tensor_scalar(
                    Q_bf[:, cc * TOWN + qc * QCW: cc * TOWN + (qc + 1) * QCW],
                    ps_proj[:, 512 * a:512 * a + QCW],
                    bq_sb[:, cc:cc + 1], None, OP.add))
        block.vector(_)
        qev_v = sV.n

        def _(q):
            q.wait_ge(sCK.h, 1)
            for cc in range(CCH):
                for r in range(GRP):
                    sGa.n += 16
                    q.dma_start(
                        out=K_res[:, cc * NKV + TOWN * r: cc * NKV + TOWN * (r + 1)],
                        in_=ccK_out[HE * r + 128 * cc: HE * r + 128 * (cc + 1), :]
                    ).then_inc(sGa.h, 16)
            q.wait_ge(sCV.h, 1)
            for kt in range(KCH):
                sGa.n += 16
                q.dma_start(out=V_res[:, kt * VROW:(kt + 1) * VROW],
                            in_=ccV_out[128 * kt:128 * (kt + 1), :]).then_inc(sGa.h, 16)
        block.sync(_)
        sL_gather = sGa.n

        if stop_after == "proj":
            psA.close()
            ctxA.close()
            return nc

        if debug_dumps:
            def _(g):
                g.wait_ge(sGa.h, sL_gather)
                g.wait_ge(sV.h, qev_v)
                for dst, srct in [(dbg_K, K_res), (dbg_V, V_res), (dbg_Q, Q_bf)]:
                    sG.n += 16
                    g.dma_start(out=dst[:, :], in_=srct[:, :]).then_inc(sG.h, 16)
                g.wait_ge(sG.h, sG.n)
            block.gpsimd(_)

        psA.close()
        psA2 = ExitStack()
        ps_S = [psA2.enter_context(nc.psum_tensor(f"ps_S{i}", [128, 2 * QCW], F32))
                for i in range(2)]
        ps_PV = [psA2.enter_context(nc.psum_tensor(f"ps_PV{i}", [128, 512], F32))
                 for i in range(2)]
        ps_misc = psA2.enter_context(nc.psum_tensor("ps_misc", [128, 512], F32))

        last_exp_psS = [0, 0]
        last_pv_Pbf = [sT.n, sT.n]
        evac_done = {-2: sV.n, -1: sV.n}
        negc_ready = {}
        misc_free = sV.n
        last_cmT = [sT.n]
        first_att = True

        for qc in range(NQC):
            for h in range(H):
                idx = qc * H + h
                hb = idx % 2
                cc = h // 2
                rb = 64 * (h % 2)
                need_c = qc == 0
                pv_cnt = {}
                exp_cnt = {}
                s_cnt = {}

                def q_rhs(qc=qc, cc=cc, rb=rb):
                    return Q_bf[rb:rb + E,
                                cc * TOWN + qc * QCW: cc * TOWN + (qc + 1) * QCW]

                def s_lhs(k, cc=cc, rb=rb):
                    return K_res[rb:rb + E,
                                 cc * NKV + 128 * k: cc * NKV + 128 * (k + 1)]

                def pv_step(te, gp, h=h, hb=hb):
                    last = None
                    for j in range(2):
                        k = 2 * gp + j
                        last = te.matmul(
                            ps_PV[hb][0:VW, 0:QCW],
                            V_res[:, k * VROW + VW * h: k * VROW + VW * (h + 1)],
                            P_bf[gp % 2][:, j * QCW:(j + 1) * QCW],
                            start=(k == 0), stop=(k == KCH - 1))
                    pv_cnt[gp] = sT.inc(last)
                    last_pv_Pbf[gp % 2] = pv_cnt[gp]

                for g in range(NG):
                    par = g % 2
                    fa = first_att
                    first_att = False

                    def _(te, g=g, par=par, fa=fa, idx=idx):
                        if fa:
                            te.wait_ge(sGa.h, sL_gather)
                            te.wait_ge(sV.h, qev_v)
                        te.wait_ge(sC.h, last_exp_psS[par])
                        last = None
                        for j in range(2):
                            last = te.matmul(
                                ps_S[par][:, j * QCW:(j + 1) * QCW],
                                s_lhs(2 * g + j), q_rhs(), start=True, stop=True)
                        s_cnt[g] = sT.inc(last)
                        if g >= 1:
                            gp = g - 1
                            te.wait_ge(sC.h, exp_cnt[gp])
                            if gp == 0:
                                te.wait_ge(sV.h, evac_done[idx - 2])
                            pv_step(te, gp)
                    block.tensor(_)

                    if need_c and g == 0:
                        def _(dv, lcm=last_cmT[0]):
                            dv.wait_ge(sT.h, s_cnt[0])
                            dv.wait_ge(sT.h, lcm)
                            dv.reduce_max(cmax[:, 0:1], ps_S[0][:, 0:2 * QCW],
                                          axis=AX.X)
                            dv.engine_nop()
                            dv.engine_nop()
                            sV.inc(dv.tensor_copy(out=cmaxb[:, 0:1],
                                                  in_=cmax[:, 0:1]))
                        block.vector(_)
                        vv = sV.n

                        def _(te, vv=vv, mf=misc_free):
                            te.wait_ge(sV.h, vv)
                            te.wait_ge(sV.h, mf)
                            sT.inc(te.matmul(ps_misc[0:1, 128:256], cmaxb[:, 0:1],
                                             ident_bf[:, :], start=True, stop=True))
                        block.tensor(_)
                        tv = sT.n
                        last_cmT[0] = tv

                        def _(dv, tv=tv):
                            dv.wait_ge(sT.h, tv)
                            dv.reduce_max(cmax[0:1, 1:2], ps_misc[0:1, 128:256],
                                          axis=AX.X, negate=True)
                            dv.engine_nop()
                            sV.inc(dv.engine_nop())
                        block.vector(_)
                        vv2 = sV.n

                        def _(te, vv2=vv2):
                            te.wait_ge(sV.h, vv2)
                            sT.inc(te.matmul(ps_misc[:, 0:1], ones_row[0:1, :],
                                             cmax[0:1, 1:2], start=True, stop=True))
                        block.tensor(_)
                        tv2 = sT.n

                        def _(dv, h=h, tv2=tv2):
                            dv.wait_ge(sT.h, tv2)
                            sV.inc(dv.tensor_copy(out=negC[:, h:h + 1],
                                                  in_=ps_misc[:, 0:1]))
                        block.vector(_)
                        negc_ready[h] = sV.n
                        misc_free = sV.n

                    def _(sc, g=g, par=par, h=h):
                        sc.wait_ge(sT.h, s_cnt[g])
                        if g == 0:
                            sc.wait_ge(sV.h, negc_ready[h])
                        sc.wait_ge(sT.h, last_pv_Pbf[par])
                        exp_cnt[g] = sC.inc(sc.activation(
                            P_bf[par][:, 0:2 * QCW], ps_S[par][:, 0:2 * QCW],
                            AF.Exp, bias=negC[:, h:h + 1]))
                        last_exp_psS[par] = exp_cnt[g]
                    block.scalar(_)

                def _(te, idx=idx):
                    gp = NG - 1
                    te.wait_ge(sC.h, exp_cnt[gp])
                    if gp == 0:
                        te.wait_ge(sV.h, evac_done[idx - 2])
                    pv_step(te, gp)
                block.tensor(_)

                def _(dv, hb=hb):
                    dv.wait_ge(sT.h, pv_cnt[NG - 1])
                    dv.tensor_scalar(zr_t[0:1, 0:QCW], ps_PV[hb][64:65, 0:QCW],
                                     float(math.sqrt(HEAD_DIM)), 1e-30,
                                     OP.mult, OP.add)
                    dv.engine_nop()
                    dv.engine_nop()
                    dv.reciprocal(zr_t[0:1, QCW:2 * QCW], zr_t[0:1, 0:QCW])
                    dv.engine_nop()
                    sV.inc(dv.engine_nop())
                block.vector(_)
                vv3 = sV.n

                def _(te, vv3=vv3, mf=misc_free):
                    te.wait_ge(sV.h, vv3)
                    te.wait_ge(sV.h, mf)
                    sT.inc(te.matmul(ps_misc[0:64, 0:QCW], ones_row[0:1, 0:64],
                                     zr_t[0:1, QCW:2 * QCW], start=True, stop=True))
                block.tensor(_)
                tv3 = sT.n

                def _(dv, tv3=tv3, hb=hb, h=h, cc=cc, rb=rb, qc=qc):
                    dv.wait_ge(sT.h, tv3)
                    dv.tensor_copy(out=zrb_sb[:, 0:QCW], in_=ps_misc[0:64, 0:QCW])
                    dv.engine_nop()
                    dv.engine_nop()
                    sV.inc(dv.tensor_mul(
                        OnT[rb:rb + 64,
                            cc * TOWN + qc * QCW: cc * TOWN + (qc + 1) * QCW],
                        ps_PV[hb][0:64, 0:QCW], zrb_sb[:, 0:QCW]))
                block.vector(_)
                evac_done[idx] = sV.n
                misc_free = sV.n

        att_done_v = sV.n
        if debug_dumps:
            def _(g):
                g.wait_ge(sV.h, att_done_v)
                sG.n += 16
                g.dma_start(out=dbg_OnT[:, :], in_=OnT[:, :]).then_inc(sG.h, 16)
                sG.n += 16
                g.dma_start(out=dbg_negC[:, :], in_=negC[:, :]).then_inc(sG.h, 16)
                g.wait_ge(sG.h, sG.n)
            block.gpsimd(_)
        psA2.close()
        if stop_after == "attn":
            ctxA.close()
            return nc
        ctxA.close()

        ctxB = ExitStack()
        R = sb(ctxB, "R", 128, DCH * TOWN, BF16)
        NMb = [sb(ctxB, f"NMb{i}", 128, QCW, F32) for i in range(NQC)]
        Ib = [sb(ctxB, f"Ib{i}", 128, QCW, F32) for i in range(NQC)]

        ctxB1 = ExitStack()
        WOS = 2
        wo_f = [sb(ctxB1, f"wo_f{i}", 128, 128 * OGW, F32) for i in range(WOS * CCH)]
        wo_b = [sb(ctxB1, f"wo_b{i}", 128, 128 * OGW, BF16) for i in range(WOS * CCH)]
        XRN = 6
        xre = [sb(ctxB1, f"xre{i}", 128, QCW, F32) for i in range(XRN)]
        scrA = [sb(ctxB1, f"scrA{i}", 128, QCW, F32) for i in range(4)]
        y1scr = [sb(ctxB1, f"y1scr{i}", 128, QCW, BF16) for i in range(8)]

        psB1 = ExitStack()
        ps_W = [psB1.enter_context(nc.psum_tensor(f"ps_W{i}", [128, 512], F32))
                for i in range(4)]
        ps_s1 = [psB1.enter_context(nc.psum_tensor(f"ps_s1_{i}", [128, 512], F32))
                 for i in range(NQC)]
        ps_s2 = [psB1.enter_context(nc.psum_tensor(f"ps_s2_{i}", [128, 512], F32))
                 for i in range(NQC)]

        NOG = DCH // OGW
        wo_conv = {}
        wo_mm_done = {}
        scr_sc = {}
        scr_free = {-i: sV.n for i in range(1, 5)}
        y1_free = {-i: sT.n for i in range(1, 9)}
        sq_pw = {}
        SLAG = 3

        def emit_stats(oi2):
            o2_, qc2 = divmod(oi2, NQC)

            def _(te, o=o2_, qc=qc2, oi2=oi2):
                te.wait_ge(sP.h, sq_pw[oi2])
                sl = y1scr[oi2 % 8]
                te.matmul(ps_s1[qc][0:1, 0:QCW], ones_col[:, 0:1],
                          R[:, o * TOWN + qc * QCW: o * TOWN + (qc + 1) * QCW],
                          start=(o == 0), stop=(o == DCH - 1))
                last = te.matmul(ps_s2[qc][0:1, 0:QCW], ones_col[:, 0:1],
                                 sl[:, 0:QCW],
                                 start=(o == 0), stop=(o == DCH - 1))
                y1_free[oi2] = sT.inc(last)
            block.tensor(_)
        xre_free = {-i: sV.n for i in range(1, XRN + 1)}
        oi = 0
        for og in range(NOG):
            def _(q, og=og):
                if og == 0:
                    q.wait_ge(sV.h, att_done_v)
                if og >= WOS:
                    q.wait_ge(sP.h, wo_conv[og - WOS])
                for ccx in range(CCH):
                    sWo[og % WOS].n += 16
                    q.dma_start(
                        out=wo_f[(og % WOS) * CCH + ccx][:, :],
                        in_=woT[128 * ccx:128 * (ccx + 1), 128 * OGW * og:128 * OGW * (og + 1)]
                    ).then_inc(sWo[og % WOS].h, 16)
            block.sync(_)

            def _(dv, og=og):
                dv.wait_ge(sWo[og % WOS].h, 16 * CCH * (og // WOS + 1))
                if og >= WOS:
                    dv.wait_ge(sT.h, wo_mm_done[og - WOS])
                last = None
                for ccx in range(CCH):
                    last = dv.tensor_copy(out=wo_b[(og % WOS) * CCH + ccx][:, :],
                                          in_=wo_f[(og % WOS) * CCH + ccx][:, :])
                wo_conv[og] = sV.inc(last)
            block.vector(_)

            for o4 in range(OGW):
                o = og * OGW + o4
                for qc in range(NQC):
                    par = oi % 4

                    def _(te, og=og, o4=o4, o=o, qc=qc, par=par, oi=oi):
                        te.wait_ge(sV.h, wo_conv[og])
                        if oi >= 4:
                            te.wait_ge(sV.h, scr_sc[oi - 4])
                        last = None
                        for ccx in range(CCH):
                            last = te.matmul(
                                ps_W[par][:, 0:QCW],
                                wo_b[(og % WOS) * CCH + ccx][:, 128 * o4:128 * (o4 + 1)],
                                OnT[:, ccx * TOWN + qc * QCW: ccx * TOWN + (qc + 1) * QCW],
                                start=(ccx == 0), stop=(ccx == CCH - 1))
                        sT.inc(last)
                    block.tensor(_)
                    tw = sT.n

                    def _(q, o=o, qc=qc, oi=oi):
                        q.wait_ge(sV.h, xre_free[oi - XRN])
                        sXre[oi % XRN].n += 16
                        q.dma_start(
                            out=xre[oi % XRN][:, :],
                            in_=x_dT[128 * o:128 * (o + 1), qc * QCW:(qc + 1) * QCW]
                        ).then_inc(sXre[oi % XRN].h, 16)
                    block.sync(_)
                    lw = sXre[oi % XRN].n

                    def _(dv, o=o, qc=qc, par=par, oi=oi, lw=lw, tw=tw):
                        dv.wait_ge(sT.h, tw)
                        if oi >= 4:
                            dv.wait_ge(sV.h, scr_free[oi - 4])
                        dv.tensor_scalar(
                            scrA[oi % 4][:, :], ps_W[par][:, 0:QCW],
                            bo_sb[:, o:o + 1], None, OP.add)
                        dv.wait_ge(sXre[oi % XRN].h, lw)
                        sV.inc(dv.tensor_add(
                            R[:, o * TOWN + qc * QCW: o * TOWN + (qc + 1) * QCW],
                            scrA[oi % 4][:, :], xre[oi % XRN][:, :]))
                    block.vector(_)
                    scr_free[oi] = sV.n
                    scr_sc[oi] = sV.n
                    xre_free[oi] = sV.n
                    vw = sV.n

                    def _(g, o=o, qc=qc, oi=oi, vw=vw):
                        g.wait_ge(sV.h, vw)
                        if oi >= 8:
                            g.wait_ge(sT.h, y1_free[oi - 8])
                        sl = y1scr[oi % 8]
                        sP.inc(g.tensor_mul(
                            sl[:, 0:QCW],
                            R[:, o * TOWN + qc * QCW: o * TOWN + (qc + 1) * QCW],
                            R[:, o * TOWN + qc * QCW: o * TOWN + (qc + 1) * QCW]))
                    block.gpsimd(_)
                    sq_pw[oi] = sP.n

                    if oi >= SLAG:
                        emit_stats(oi - SLAG)
                    oi += 1
            wo_mm_done[og] = sT.n
        for oi2 in range(max(0, oi - SLAG), oi):
            emit_stats(oi2)

        stat_t = sT.n
        for qc in range(NQC):
            prev_sc = sC.n

            def _(dv, qc=qc, prev_sc=prev_sc):
                dv.wait_ge(sT.h, stat_t)
                dv.wait_ge(sC.h, prev_sc)
                nm = negm_r[0:1, qc * QCW:(qc + 1) * QCW]
                dv.tensor_scalar(nm, ps_s1[qc][0:1, 0:QCW], -1.0 / D, None, OP.mult)
                dv.tensor_scalar(lnscr[0:1, 0:QCW], ps_s2[qc][0:1, 0:QCW],
                                 1.0 / D, None, OP.mult)
                dv.engine_nop()
                dv.engine_nop()
                dv.tensor_mul(lnscr[0:1, QCW:2 * QCW], nm, nm)
                sV.inc(dv.scalar_tensor_tensor(lnscr[0:1, 0:QCW], lnscr[0:1, 0:QCW],
                                               1e-5, lnscr[0:1, QCW:2 * QCW],
                                               OP.add, OP.subtract))
            block.vector(_)
            vw = sV.n

            def _(sc, qc=qc, vw=vw):
                sc.wait_ge(sV.h, vw)
                sc.activation(lnscr[0:1, 0:QCW], lnscr[0:1, 0:QCW], AF.Ln)
                sC.inc(sc.activation(inv_r[0:1, qc * QCW:(qc + 1) * QCW],
                                     lnscr[0:1, 0:QCW], AF.Exp, scale=-0.5))
            block.scalar(_)
        ln1_c = sC.n

        if debug_dumps:
            def _(g, c0=ln1_c):
                g.wait_ge(sC.h, c0)
                sG.n += 16
                g.dma_start(out=dbg_R[:, :], in_=R[:, :]).then_inc(sG.h, 16)
                sG.n += 16
                g.dma_start(out=dbg_rows[0:1, 0:NQC * QCW],
                            in_=negm_r[0:1, :]).then_inc(sG.h, 16)
                sG.n += 16
                g.dma_start(out=dbg_rows[0:1, NQC * QCW:2 * NQC * QCW],
                            in_=inv_r[0:1, :]).then_inc(sG.h, 16)
                g.wait_ge(sG.h, sG.n)
            block.gpsimd(_)
        psB1.close()
        ctxB1.close()
        ctxOn.close()

        ctxB2 = ExitStack()
        x1f = sb(ctxB2, "x1f", 128, DCH * TOWN, BF16)
        gw_bf = sb(ctxB2, "gw_bf", 128, NE * DCH, BF16)
        FW = len(sFw)
        fw_f = [sb(ctxB2, f"fw_f{i}", 128, 256, F32) for i in range(FW)]
        fw_b = [sb(ctxB2, f"fw_b{i}", 128, 256, BF16) for i in range(FW)]
        gscr = [sb(ctxB2, f"gscr{i}", 128, QCW, F32) for i in range(2)]
        LW = sb(ctxB2, "LW", 128, 512, F32)
        GA = sb(ctxB2, "GA", 128, 512, F32)
        GB = sb(ctxB2, "GB", 128, 512, F32)
        LW2 = sb(ctxB2, "LW2", 128, 512, F32)
        GA2 = sb(ctxB2, "GA2", 128, 512, F32)
        GB2 = sb(ctxB2, "GB2", 128, 512, F32)
        gu_sb = sb(ctxB2, "gu_sb", NE, QCW, F32)
        wch_sb = sb(ctxB2, "wch_sb", NE, QCW, F32)
        wb_sb = [sb(ctxB2, f"wb_sb{i}", 128, QCW, F32) for i in range(NQC)]
        low_bf = [sb(ctxB2, f"low_bf{i}", 128, QCW, BF16) for i in range(NQC)]
        wb3 = sb(ctxB2, "wb3", 128, NE, BF16)
        wb3b = sb(ctxB2, "wb3b", 128, NE, BF16)
        tcol = sb(ctxB2, "tcol", 128, 16, F32)
        outst = ([gscr[0], gscr[1], LW, GA, GB]
                 + ([Ib[1], wb_sb[0], wb_sb[1]] if NQC == 2 else [wb_sb[0]]))
        NOUT = len(outst)
        evsc = [gscr[0], gscr[1], LW, GA]
        NM2b, I2b = NMb[0], Ib[0]

        psB2a = ExitStack()
        ps_m1 = psB2a.enter_context(nc.psum_tensor("ps_m1", [128, 512], F32))
        ps_m2 = psB2a.enter_context(nc.psum_tensor("ps_m2", [128, 512], F32))

        b1_t, b1_v, b1_c, b1_p = sT.n, sV.n, sC.n, sP.n

        fw_uses = [0] * FW

        def _(q):
            q.wait_ge(sT.h, b1_t)
            q.wait_ge(sV.h, b1_v)
            q.wait_ge(sC.h, b1_c)
            q.wait_ge(sP.h, b1_p)
            fw_uses[0] += 1
            sFw[0].n += 16
            q.dma_start(out=fw_f[0][:, 0:NE * DCH],
                        in_=gw_pk[:, :]).then_inc(sFw[0].h, 16)
        block.sync(_)
        lw = sFw[0].n

        def _(g, lw=lw):
            g.wait_ge(sFw[0].h, lw)
            sP.inc(g.tensor_copy(out=gw_bf[:, :], in_=fw_f[0][:, 0:NE * DCH]))
        block.gpsimd(_)
        gw_p = sP.n

        fst = {"j": 0, "conv": {}, "cons": {}, "load": {}}

        def cwait(eng, j):
            pe, pc = fst["conv"][j]
            eng.wait_ge(sV.h if pe == "v" else sP.h, pc)

        def fw_item(src_ap, width, eng="g"):
            j = fst["j"]
            slot = j % FW

            def _(q, j=j, slot=slot, src_ap=src_ap, width=width):
                if j >= FW:
                    cwait(q, j - FW)
                else:
                    q.wait_ge(sP.h, gw_p)
                fw_uses[slot] += 1
                fst["load"][j] = 16 * fw_uses[slot]
                sFw[slot].n += 16
                q.dma_start(out=fw_f[slot][:, 0:width],
                            in_=src_ap).then_inc(sFw[slot].h, 16)
            block.sync(_)

            if eng == "g":
                def _(g, j=j, slot=slot, width=width):
                    g.wait_ge(sFw[slot].h, fst["load"][j])
                    if j >= FW:
                        g.wait_ge(sT.h, fst["cons"][j - FW])
                    fst["conv"][j] = ("g", sP.inc(g.tensor_copy(
                        out=fw_b[slot][:, 0:width], in_=fw_f[slot][:, 0:width])))
                block.gpsimd(_)
            else:
                def _(dv, j=j, slot=slot, width=width):
                    dv.wait_ge(sFw[slot].h, fst["load"][j])
                    if j >= FW:
                        dv.wait_ge(sT.h, fst["cons"][j - FW])
                    fst["conv"][j] = ("v", sV.inc(dv.tensor_copy(
                        out=fw_b[slot][:, 0:width], in_=fw_f[slot][:, 0:width])))
                block.vector(_)
            fst["j"] = j + 1
            return j, slot

        def fst_done(j, cnt):
            fst["cons"][j] = cnt

        bc_v = [None] * NQC
        for qcx in range(NQC):
            def _(te, qcx=qcx):
                te.wait_ge(sC.h, ln1_c)
                if qcx > 0:
                    te.wait_ge(sV.h, bc_v[qcx - 1])
                te.matmul(ps_m1[:, 0:QCW], ones_row[0:1, :],
                          negm_r[0:1, qcx * QCW:(qcx + 1) * QCW],
                          start=True, stop=True)
                sT.inc(te.matmul(ps_m2[:, 0:QCW], ones_row[0:1, :],
                                 inv_r[0:1, qcx * QCW:(qcx + 1) * QCW],
                                 start=True, stop=True))
            block.tensor(_)
            tw = sT.n

            def _(dv, qcx=qcx, tw=tw):
                dv.wait_ge(sT.h, tw)
                dv.tensor_copy(out=NMb[qcx][:, :], in_=ps_m1[:, 0:QCW])
                dv.tensor_copy(out=Ib[qcx][:, :], in_=ps_m2[:, 0:QCW])
                dv.engine_nop()
                bc_v[qcx] = sV.inc(dv.engine_nop())
            block.vector(_)

        x1c = [[None] * DCH for _ in range(NQC)]

        def xwait(eng, qcx, dc):
            pe, pc = x1c[qcx][dc]
            eng.wait_ge(sV.h if pe == "v" else sP.h, pc)

        for qcx in range(NQC):
            gp_dcs = [dc for dc in range(DCH) if dc % 4 == 3]
            for dc in [d for d in range(DCH) if d % 4 != 3]:
                def _(dv, qcx=qcx, dc=dc):
                    ch = slice(dc * TOWN + qcx * QCW, dc * TOWN + (qcx + 1) * QCW)
                    dv.tensor_add(gscr[0][:, :], R[:, ch], NMb[qcx][:, :])
                    x1c[qcx][dc] = ("v", sV.inc(dv.tensor_mul(
                        x1f[:, ch], gscr[0][:, :], Ib[qcx][:, :])))
                block.vector(_)
            for i, dc in enumerate(gp_dcs):
                def _(g, qcx=qcx, dc=dc, i=i):
                    if i == 0:
                        g.wait_ge(sV.h, bc_v[qcx])
                    ch = slice(dc * TOWN + qcx * QCW, dc * TOWN + (qcx + 1) * QCW)
                    g.tensor_add(gscr[1][:, :], R[:, ch], NMb[qcx][:, :])
                    x1c[qcx][dc] = ("g", sP.inc(g.tensor_mul(
                        x1f[:, ch], gscr[1][:, :], Ib[qcx][:, :])))
                block.gpsimd(_)
        if debug_dumps:
            def _(g):
                xwait(g, NQC - 1, DCH - 1)
                xwait(g, NQC - 1, DCH - 2)
                sG.n += 16
                g.dma_start(out=dbg_x1q[:, :], in_=x1f[:, :]).then_inc(sG.h, 16)
                g.wait_ge(sG.h, sG.n)
            block.gpsimd(_)

        NT4 = QCW // 128
        wbv = [None] * NQC
        for qcx in range(NQC):
            def _(te, qcx=qcx):
                te.wait_ge(sP.h, gw_p)
                if qcx > 0:
                    te.wait_ge(sV.h, wbv[qcx - 1])
                last = None
                for dc in range(DCH):
                    xwait(te, qcx, dc)
                    last = te.matmul(ps_m1[0:NE, 0:QCW],
                                     gw_bf[:, NE * dc:NE * (dc + 1)],
                                     x1f[:, dc * TOWN + qcx * QCW:
                                         dc * TOWN + (qcx + 1) * QCW],
                                     start=(dc == 0), stop=(dc == DCH - 1))
                sT.inc(last)
            block.tensor(_)
            tw = sT.n

            def _(dv, tw=tw):
                dv.wait_ge(sT.h, tw)
                sV.inc(dv.tensor_copy(out=gu_sb[:, :], in_=ps_m1[0:NE, 0:QCW]))
            block.vector(_)
            vw = sV.n

            def _(te, vw=vw):
                te.wait_ge(sV.h, vw)
                last = None
                for t4 in range(NT4):
                    last = te.matmul(ps_m2[:, 8 * t4:8 * (t4 + 1)],
                                     gu_sb[0:NE, 128 * t4:128 * (t4 + 1)],
                                     ident_f[0:NE, 0:NE], start=True, stop=True)
                sT.inc(last)
            block.tensor(_)
            tw2 = sT.n

            if debug_dumps and qcx == NQC - 1:
                def _(g, tw2=tw2):
                    g.wait_ge(sT.h, tw2)
                    sG.n += 16
                    g.dma_start(out=dbg_gu[:, :], in_=gu_sb[:, :]).then_inc(sG.h, 16)
                    g.wait_ge(sG.h, sG.n)
                block.gpsimd(_)

            LWs, GAs, GBs, wb3s = [LW, LW2], [GA, GA2], [GB, GB2], [wb3, wb3b]
            wb3_free = {0: 0, 1: 0}
            sig_cw = {}

            def t4_part1(t4, tw2):
                p = t4 % 2
                tb = 8 * p

                def _(dv, t4=t4, tw2=tw2, p=p, tb=tb, lwb=wb3_free[p]):
                    dv.wait_ge(sT.h, tw2)
                    dv.wait_ge(sT.h, lwb)
                    dv.tensor_copy(out=LWs[p][:, 0:NE],
                                   in_=ps_m2[:, 8 * t4:8 * (t4 + 1)])
                    dv.memset(LWs[p][:, NE:512], -3e38)
                    dv.engine_nop()
                    dv.engine_nop()
                    dv.reduce_max(tcol[:, tb + 0:tb + 1], LWs[p][:, :], axis=AX.X)
                    dv.engine_nop()
                    dv.engine_nop()
                    dv.tensor_scalar(GAs[p][:, :], LWs[p][:, :],
                                     tcol[:, tb + 0:tb + 1], None, OP.is_equal)
                    dv.scalar_tensor_tensor(GBs[p][:, :], GAs[p][:, :],
                                            -1e30, LWs[p][:, :], OP.mult, OP.add)
                    dv.engine_nop()
                    dv.engine_nop()
                    dv.reduce_max(tcol[:, tb + 2:tb + 3], GBs[p][:, :], axis=AX.X)
                    dv.engine_nop()
                    dv.engine_nop()
                    dv.tensor_scalar(GBs[p][:, :], LWs[p][:, :],
                                     tcol[:, tb + 2:tb + 3], None, OP.is_ge)
                    dv.tensor_sub(GBs[p][:, :], GBs[p][:, :], GAs[p][:, :])
                    dv.tensor_sub(tcol[:, tb + 3:tb + 4], tcol[:, tb + 0:tb + 1],
                                  tcol[:, tb + 2:tb + 3])
                    sV.inc(dv.tensor_sub(tcol[:, tb + 4:tb + 5],
                                         tcol[:, tb + 2:tb + 3],
                                         tcol[:, tb + 0:tb + 1]))
                block.vector(_)
                vw2 = sV.n

                def _(sc, tb=tb, vw2=vw2):
                    sc.wait_ge(sV.h, vw2)
                    sc.activation(tcol[:, tb + 5:tb + 6], tcol[:, tb + 3:tb + 4],
                                  AF.Sigmoid)
                    sig_cw[t4] = sC.inc(sc.activation(
                        tcol[:, tb + 6:tb + 7], tcol[:, tb + 4:tb + 5],
                        AF.Sigmoid))
                block.scalar(_)

            def t4_part2(t4):
                p = t4 % 2
                tb = 8 * p

                def _(dv, t4=t4, p=p, tb=tb):
                    dv.wait_ge(sC.h, sig_cw[t4])
                    dv.tensor_scalar(GAs[p][:, :], GAs[p][:, :],
                                     tcol[:, tb + 5:tb + 6], None, OP.mult)
                    dv.tensor_scalar(GBs[p][:, :], GBs[p][:, :],
                                     tcol[:, tb + 6:tb + 7], None, OP.mult)
                    dv.tensor_add(GAs[p][:, :], GAs[p][:, :], GBs[p][:, :])
                    sV.inc(dv.tensor_copy(out=wb3s[p][:, :], in_=GAs[p][:, 0:NE]))
                block.vector(_)
                vw3 = sV.n

                def _(te, t4=t4, p=p, vw3=vw3):
                    te.wait_ge(sV.h, vw3)
                    sT.inc(te.matmul(ps_m1[0:NE, 128 * t4:128 * (t4 + 1)],
                                     wb3s[p][:, :], ident_bf[:, :],
                                     start=True, stop=True))
                block.tensor(_)
                wb3_free[p] = sT.n

            for t4 in range(NT4):
                t4_part1(t4, tw2)
                if t4 >= 1:
                    t4_part2(t4 - 1)
            t4_part2(NT4 - 1)
            tw3 = sT.n

            def _(dv, tw3=tw3):
                dv.wait_ge(sT.h, tw3)
                sV.inc(dv.tensor_copy(out=wch_sb[:, :], in_=ps_m1[0:NE, 0:QCW]))
            block.vector(_)
            vw4 = sV.n

            def _(te, vw4=vw4):
                te.wait_ge(sV.h, vw4)
                sT.inc(te.matmul(ps_m2[:, 0:QCW], sel_f[:, :], wch_sb[:, :],
                                 start=True, stop=True))
            block.tensor(_)
            tw4 = sT.n

            def _(dv, tw4=tw4, qcx=qcx):
                dv.wait_ge(sT.h, tw4)
                wbv[qcx] = sV.inc(dv.tensor_copy(out=wb_sb[qcx][:, :],
                                                 in_=ps_m2[:, 0:QCW]))
            block.vector(_)
            if debug_dumps and qcx == NQC - 1:
                def _(g, qcx=qcx):
                    g.wait_ge(sV.h, wbv[qcx])
                    sG.n += 16
                    g.dma_start(out=dbg_wch[:, :], in_=wch_sb[:, :]).then_inc(sG.h, 16)
                    g.wait_ge(sG.h, sG.n)
                block.gpsimd(_)

        for dc in range(DCH):
            j, slot = fw_item(la_pk[:, 128 * dc:128 * (dc + 1)], 128)

            def _(te, dc=dc, j=j, slot=slot):
                cwait(te, j)
                if dc == 0:
                    te.wait_ge(sV.h, wbv[NQC - 1])
                last = None
                for qcx in range(NQC):
                    pb = ps_m1 if qcx == 0 else ps_m2
                    last = te.matmul(pb[:, 0:QCW], fw_b[slot][:, 0:128],
                                     x1f[:, dc * TOWN + qcx * QCW:
                                         dc * TOWN + (qcx + 1) * QCW],
                                     start=(dc == 0), stop=(dc == DCH - 1))
                fst_done(j, sT.inc(last))
            block.tensor(_)
        tw5 = sT.n

        def _(dv, tw5=tw5):
            dv.wait_ge(sT.h, tw5)
            for qcx in range(NQC):
                pb = ps_m1 if qcx == 0 else ps_m2
                sV.inc(dv.tensor_mul(low_bf[qcx][:, :], pb[:, 0:QCW],
                                     wb_sb[qcx][:, :]))
        block.vector(_)
        low_v = sV.n
        if debug_dumps:
            def _(g):
                g.wait_ge(sV.h, low_v)
                sG.n += 16
                g.dma_start(out=dbg_low[:, :], in_=low_bf[0][:, :]).then_inc(sG.h, 16)
                g.wait_ge(sG.h, sG.n)
            block.gpsimd(_)

        psB2a.close()
        psB2b = ExitStack()
        NB = 2 * NQC
        ps_B = [psB2b.enter_context(nc.psum_tensor(f"ps_B{i}", [128, 512], F32))
                for i in range(2 * NB)]

        bw_done = {}
        ev_v = {-2: sV.n, -1: sV.n}
        NOB = DCH // 2

        def emit_evac(ob):
            def _(dv, ob=ob):
                dv.wait_ge(sT.h, bw_done[ob])
                for o2 in range(2):
                    for qcx in range(NQC):
                        ei = o2 * NQC + qcx
                        dv.tensor_scalar(evsc[ei][:, 0:QCW],
                                         ps_B[(ob % 2) * NB + ei][:, 0:QCW],
                                         bb_sb[:, ob * 2 + o2:ob * 2 + o2 + 1],
                                         None, OP.add)
                for o2 in range(2):
                    for qcx in range(NQC):
                        ei = o2 * NQC + qcx
                        o = ob * 2 + o2
                        ch = slice(o * TOWN + qcx * QCW,
                                   o * TOWN + (qcx + 1) * QCW)
                        sV.inc(dv.tensor_add(R[:, ch], x1f[:, ch],
                                             evsc[ei][:, 0:QCW]))
            block.vector(_)
            ev_v[ob] = sV.n

        for ob in range(NOB):
            for dc in range(DCH):
                j, slot = fw_item(
                    bwT[128 * dc:128 * (dc + 1), 256 * ob:256 * (ob + 1)], 256,
                    eng="v" if dc % 2 else "g")

                def _(te, j=j, slot=slot, ob=ob, dc=dc):
                    cwait(te, j)
                    if dc == 0:
                        te.wait_ge(sV.h, low_v)
                        te.wait_ge(sV.h, ev_v[ob - 2])
                    last = None
                    for o2 in range(2):
                        for qcx in range(NQC):
                            last = te.matmul(
                                ps_B[(ob % 2) * NB + o2 * NQC + qcx][:, 0:QCW],
                                fw_b[slot][:, 128 * o2:128 * (o2 + 1)],
                                x1f[:, dc * TOWN + qcx * QCW:
                                    dc * TOWN + (qcx + 1) * QCW],
                                start=(dc == 0), stop=False)
                    fst_done(j, sT.inc(last))
                block.tensor(_)

            j, slot = fw_item(lb_pk[0:ER, 256 * ob:256 * (ob + 1)], 256)

            def _(te, j=j, slot=slot, ob=ob):
                cwait(te, j)
                last = None
                for o2 in range(2):
                    for qcx in range(NQC):
                        last = te.matmul(
                            ps_B[(ob % 2) * NB + o2 * NQC + qcx][:, 0:QCW],
                            fw_b[slot][0:ER, 128 * o2:128 * (o2 + 1)],
                            low_bf[qcx][:, :], start=False, stop=True)
                fst_done(j, sT.inc(last))
                bw_done[ob] = sT.n
            block.tensor(_)

            if ob >= 1:
                emit_evac(ob - 1)
        emit_evac(NOB - 1)

        psB2b.close()
        psB2c = ExitStack()
        ps_st = [psB2c.enter_context(nc.psum_tensor(f"ps_st{i}", [128, 512], F32))
                 for i in range(2 * NQC)]
        ps_n1 = psB2c.enter_context(nc.psum_tensor("ps_n1", [128, 512], F32))
        ps_n2 = psB2c.enter_context(nc.psum_tensor("ps_n2", [128, 512], F32))

        sq_v = {}

        def _(dv):
            dv.wait_ge(sT.h, bw_done[NOB - 1])
            for dc in range(DCH):
                for qcx in range(NQC):
                    ch = slice(dc * TOWN + qcx * QCW, dc * TOWN + (qcx + 1) * QCW)
                    sq_v[(dc, qcx)] = sV.inc(dv.tensor_mul(
                        x1f[:, ch], R[:, ch], R[:, ch]))
        block.vector(_)

        st_done = [None] * NQC

        def _(te):
            te.wait_ge(sV.h, ev_v[NOB - 1])
            for dc in range(DCH):
                for qcx in range(NQC):
                    ch = slice(dc * TOWN + qcx * QCW, dc * TOWN + (qcx + 1) * QCW)
                    te.wait_ge(sV.h, sq_v[(dc, qcx)])
                    te.matmul(ps_st[2 * qcx][0:1, 0:QCW], ones_col[:, 0:1],
                              R[:, ch], start=(dc == 0), stop=(dc == DCH - 1))
                    last = te.matmul(ps_st[2 * qcx + 1][0:1, 0:QCW],
                                     ones_col[:, 0:1], x1f[:, ch],
                                     start=(dc == 0), stop=(dc == DCH - 1))
                    if dc == DCH - 1:
                        st_done[qcx] = sT.inc(last)
        block.tensor(_)

        SA = lnscr[0:1, 0:QCW]
        SB = lnscr[0:1, QCW:2 * QCW]
        SC2 = lnscr[0:1, 2 * QCW:3 * QCW]
        bcT = [None] * NQC
        out_cnt = 0
        prev_c2 = sC.n
        apv_prev = sV.n
        app_prev = sP.n
        for qcx in range(NQC):
            def _(dv, qcx=qcx, pc=prev_c2):
                dv.wait_ge(sT.h, st_done[qcx])
                dv.wait_ge(sC.h, pc)
                if qcx > 0:
                    dv.wait_ge(sT.h, bcT[qcx - 1])
                dv.tensor_scalar(SA, ps_st[2 * qcx + 1][0:1, 0:QCW], 1.0 / D,
                                 None, OP.mult)
                dv.tensor_scalar(SC2, ps_st[2 * qcx][0:1, 0:QCW], -1.0 / D,
                                 None, OP.mult)
                dv.engine_nop()
                dv.engine_nop()
                dv.tensor_mul(SB, SC2, SC2)
                sV.inc(dv.scalar_tensor_tensor(SA, SA, 1e-5, SB,
                                               OP.add, OP.subtract))
            block.vector(_)
            vw6 = sV.n

            def _(sc, vw6=vw6):
                sc.wait_ge(sV.h, vw6)
                sc.activation(SA, SA, AF.Ln)
                sC.inc(sc.activation(SB, SA, AF.Exp, scale=-0.5))
            block.scalar(_)
            cw2 = sC.n
            prev_c2 = cw2

            def _(te, qcx=qcx, cw2=cw2, apv=apv_prev, app=app_prev):
                te.wait_ge(sC.h, cw2)
                if qcx > 0:
                    te.wait_ge(sV.h, apv)
                    te.wait_ge(sP.h, app)
                te.matmul(ps_n1[:, 0:QCW], ones_row[0:1, :], SC2,
                          start=True, stop=True)
                bcT[qcx] = sT.inc(te.matmul(ps_n2[:, 0:QCW], ones_row[0:1, :],
                                            SB, start=True, stop=True))
            block.tensor(_)
            tw6 = sT.n

            def _(dv, tw6=tw6):
                dv.wait_ge(sT.h, tw6)
                dv.tensor_copy(out=NM2b[:, :], in_=ps_n1[:, 0:QCW])
                dv.tensor_copy(out=I2b[:, :], in_=ps_n2[:, 0:QCW])
                dv.engine_nop()
                sV.inc(dv.engine_nop())
            block.vector(_)
            bc2_v = sV.n

            for dc in range(DCH):
                ci = out_cnt + dc
                slot = ci % NOUT
                u = ci // NOUT
                ch = slice(dc * TOWN + qcx * QCW, dc * TOWN + (qcx + 1) * QCW)
                if ci % 4 == 3:
                    def _(g, slot=slot, u=u, ch=ch, bc2_v=bc2_v):
                        g.wait_ge(sV.h, bc2_v)
                        if u >= 1:
                            g.wait_ge(sOu[slot].h, 16 * u)
                        g.tensor_add(outst[slot][:, :], R[:, ch], NM2b[:, :])
                        sP.inc(g.tensor_mul(outst[slot][:, :],
                                            outst[slot][:, :], I2b[:, :]))
                    block.gpsimd(_)
                    prod = (sP.h, sP.n)
                    app_prev = sP.n
                else:
                    def _(dv, slot=slot, u=u, ch=ch):
                        if u >= 1:
                            dv.wait_ge(sOu[slot].h, 16 * u)
                        dv.tensor_add(outst[slot][:, :], R[:, ch], NM2b[:, :])
                        sV.inc(dv.tensor_mul(outst[slot][:, :],
                                             outst[slot][:, :], I2b[:, :]))
                    block.vector(_)
                    prod = (sV.h, sV.n)
                    apv_prev = sV.n

                def _(q, dc=dc, qcx=qcx, slot=slot, prod=prod):
                    q.wait_ge(prod[0], prod[1])
                    sOu[slot].n += 16
                    q.dma_start(out=out_dT[128 * dc:128 * (dc + 1),
                                           qcx * QCW:(qcx + 1) * QCW],
                                in_=outst[slot][:, :]).then_inc(sOu[slot].h, 16)
                block.sync(_)
            out_cnt += DCH

        psB2c.close()
        ctxB2.close()
        ctxB.close()

        def _(g):
            for s in sOu:
                if s.n:
                    g.wait_ge(s.h, s.n)
            if debug_dumps:
                sG.n += 16
                g.dma_start(out=dbg_y2[:, :], in_=R[:, :]).then_inc(sG.h, 16)
                g.wait_ge(sG.h, sG.n)
        block.gpsimd(_)
    return nc


def host_layout(inputs, D, NKV, TOWN, H):
    E = 64
    HE = H * E
    CCH = HE // 128
    DCH = D // 128
    x = np.asarray(inputs["L0_attn_x"], dtype=np.float32)
    wqk = np.asarray(inputs["L0_attn_Wqk_w"], dtype=np.float32)
    bqk = np.asarray(inputs["L0_attn_Wqk_b"], dtype=np.float32)
    wv = np.asarray(inputs["L0_attn_Wv_w"], dtype=np.float32)
    bv = np.asarray(inputs["L0_attn_Wv_b"], dtype=np.float32)
    wo = np.asarray(inputs["L0_attn_Wo_w"], dtype=np.float32)
    bo = np.asarray(inputs["L0_attn_Wo_b"], dtype=np.float32)
    gw = np.asarray(inputs["L0_ffn_gate_w"], dtype=np.float32)
    bw = np.asarray(inputs["L0_ffn_base_w"], dtype=np.float32)
    bb = np.asarray(inputs["L0_ffn_base_b"], dtype=np.float32)
    la = np.asarray(inputs["L0_ffn_lora_A"], dtype=np.float32)
    lb = np.asarray(inputs["L0_ffn_lora_B"], dtype=np.float32)

    C = np.ascontiguousarray
    shared = dict(
        wqT=C(wqk.T[:, 0::2]),
        wkT=C(wqk.T[:, 1::2]),
        bq_r=C(bqk[0::2].reshape(CCH, 128)),
        bk_r=C(bqk[1::2].reshape(CCH, 128)),
        wvT=C(wv.T),
        bv_r=C(bv.reshape(1, HE)),
        woT=C(wo.T),
        bo_r=C(bo.reshape(DCH, 128)),
        gw_pk=C(gw.T.reshape(DCH, 128, NE).transpose(1, 0, 2).reshape(128, NE * DCH)),
        bwT=C(bw.T),
        bb_r=C(bb.reshape(DCH, 128)),
        la_pk=C(la.reshape(ER, D).T.reshape(DCH, 128, ER).transpose(1, 0, 2)
                .reshape(128, ER * DCH)),
        lb_pk=C(lb.transpose(0, 2, 1).reshape(ER, D)),
    )
    in_maps = []
    for c in range(8):
        b = c // 4
        s = c % 4
        m = dict(shared)
        m["x_dT"] = C(x[b, TOWN * s:TOWN * (s + 1), :].T)
        in_maps.append(m)
    return in_maps


def kernel(**inputs):
    D, NKV, TOWN, H = 4096, 4096, 1024, 8
    nc = build_program(D, NKV, TOWN, H)
    in_maps = host_layout(inputs, D, NKV, TOWN, H)
    res = run_bass_kernel_spmd(nc, in_maps, core_ids=list(range(8)))
    out = np.empty((2, NKV, D), dtype=np.float32)
    for c in range(8):
        b, s = c // 4, c % 4
        out[b, TOWN * s:TOWN * (s + 1), :] = res.results[c]["out_dT"].T
    return out


# version 18
# speedup vs baseline: 1.1790x; 1.1745x over previous
import math
from contextlib import ExitStack

import numpy as np
import ml_dtypes
import concourse.bass as bass
import concourse.mybir as mybir
from concourse.bass_utils import run_bass_kernel_spmd

F32 = mybir.dt.float32
BF16 = mybir.dt.bfloat16
AF = mybir.ActivationFunctionType
OP = mybir.AluOpType
AX = mybir.AxisListType
bfnp = ml_dtypes.bfloat16

NE = 8
RL = 16
ER = NE * RL
LORA_SCALING = 2.0


class Sem:
    def __init__(self, nc, ctx, name):
        self.h = ctx.enter_context(nc.semaphore(name))
        self.n = 0

    def inc(self, instr, v=1):
        instr.then_inc(self.h, v)
        self.n += v
        return self.n


def build_program(D, NKV, TOWN, H, detect_races=True, stop_after=None,
                  debug_dumps=False):
    E = 64
    HE = H * E
    HEAD_DIM = D // H
    CCH = HE // 128
    DCH = D // 128
    KCH = NKV // 128
    TT = TOWN // 128
    NQC = max(1, TOWN // 512)
    QCW = TOWN // NQC
    VW = E + 1
    VROW = H * VW
    NG = KCH // 2
    GRP = 4
    assert HE % 128 == 0 and D % 128 == 0 and NKV % 128 == 0 and TOWN % 128 == 0
    assert KCH % 2 == 0 and QCW <= 512 and HE <= 512
    OGW = 4 if DCH % 4 == 0 else DCH
    assert DCH % OGW == 0 and DCH % 2 == 0

    nc = bass.Bass("TRN2", target_bir_lowering=False,
                   detect_race_conditions=detect_races)

    x_dT = nc.dram_tensor("x_dT", [D, TOWN], F32, kind="ExternalInput")
    wqT = nc.dram_tensor("wqT", [D, HE], F32, kind="ExternalInput")
    wkT = nc.dram_tensor("wkT", [D, HE], F32, kind="ExternalInput")
    wvT = nc.dram_tensor("wvT", [D, HE], F32, kind="ExternalInput")
    bq_r = nc.dram_tensor("bq_r", [CCH, 128], F32, kind="ExternalInput")
    bk_r = nc.dram_tensor("bk_r", [CCH, 128], F32, kind="ExternalInput")
    bv_r = nc.dram_tensor("bv_r", [1, HE], F32, kind="ExternalInput")
    woT = nc.dram_tensor("woT", [HE, D], F32, kind="ExternalInput")
    bo_r = nc.dram_tensor("bo_r", [DCH, 128], F32, kind="ExternalInput")
    gw_pk = nc.dram_tensor("gw_pk", [128, NE * DCH], F32, kind="ExternalInput")
    bwT = nc.dram_tensor("bwT", [D, D], F32, kind="ExternalInput")
    bb_r = nc.dram_tensor("bb_r", [DCH, 128], F32, kind="ExternalInput")
    la_pk = nc.dram_tensor("la_pk", [128, ER * DCH], F32, kind="ExternalInput")
    lb_pk = nc.dram_tensor("lb_pk", [ER, D], F32, kind="ExternalInput")
    out_dT = nc.dram_tensor("out_dT", [D, TOWN], F32, kind="ExternalOutput")
    if debug_dumps:
        dbg_OnT = nc.dram_tensor("dbg_OnT", [128, CCH * TOWN], BF16, kind="ExternalOutput")
        dbg_negC = nc.dram_tensor("dbg_negC", [128, H], F32, kind="ExternalOutput")
        dbg_rows = nc.dram_tensor("dbg_rows", [1, 2 * NQC * QCW], F32, kind="ExternalOutput")
        dbg_R = nc.dram_tensor("dbg_R", [128, DCH * TOWN], BF16, kind="ExternalOutput")
        dbg_x1q = nc.dram_tensor("dbg_x1q", [128, DCH * TOWN], BF16, kind="ExternalOutput")
        dbg_wch = nc.dram_tensor("dbg_wch", [NE, QCW], F32, kind="ExternalOutput")
        dbg_low = nc.dram_tensor("dbg_low", [128, QCW], BF16, kind="ExternalOutput")
        dbg_K = nc.dram_tensor("dbg_K", [128, CCH * NKV], BF16, kind="ExternalOutput")
        dbg_gu = nc.dram_tensor("dbg_gu", [NE, QCW], F32, kind="ExternalOutput")
        dbg_glt = nc.dram_tensor("dbg_glt", [128, 8 * 4], F32, kind="ExternalOutput")
        dbg_y2 = nc.dram_tensor("dbg_y2", [128, DCH * TOWN], BF16, kind="ExternalOutput")
        dbg_tc = nc.dram_tensor("dbg_tc", [128, 8], F32, kind="ExternalOutput")
        dbg_x1q0 = nc.dram_tensor("dbg_x1q0", [128, DCH * QCW], BF16, kind="ExternalOutput")
        dbg_V = nc.dram_tensor("dbg_V", [128, KCH * VROW], BF16, kind="ExternalOutput")
        dbg_Q = nc.dram_tensor("dbg_Q", [128, CCH * TOWN], BF16, kind="ExternalOutput")

    ccK_in = nc.dram_tensor("ccK_in", [HE, TOWN], BF16, kind="Internal")
    ccK_out = nc.dram_tensor("ccK_out", [GRP * HE, TOWN], BF16, kind="Internal")
    ccV_in = nc.dram_tensor("ccV_in", [TOWN, VROW], BF16, kind="Internal")
    ccV_out = nc.dram_tensor("ccV_out", [GRP * TOWN, VROW], BF16, kind="Internal")

    ident_bf_d = nc.inline_tensor(np.eye(128, dtype=bfnp), "c_ident_bf")
    ident_f_d = nc.inline_tensor(np.eye(128, dtype=np.float32), "c_ident_f")
    ones_row_d = nc.inline_tensor(np.ones((1, 128), dtype=np.float32), "c_ones_row")
    ones_col_d = nc.inline_tensor(np.ones((128, 1), dtype=bfnp), "c_ones_col")
    sel_d = nc.inline_tensor(
        LORA_SCALING * np.kron(np.eye(NE, dtype=np.float32),
                               np.ones((1, RL), dtype=np.float32)),
        "c_sel_f")

    est = ExitStack()
    with est:
        def sb(ctx, name, p, f, dtype):
            return ctx.enter_context(nc.sbuf_tensor(name, [p, f], dtype))

        ident_bf = sb(est, "ident_bf_s", 128, 128, BF16)
        ident_f = sb(est, "ident_f_s", 128, 128, F32)
        ones_row = sb(est, "ones_row_s", 1, 128, F32)
        ones_col = sb(est, "ones_col_s", 128, 1, BF16)
        sel_f = sb(est, "sel_s", NE, 128, F32)
        bq_sb = sb(est, "bq_sb", 128, CCH, F32)
        bk_sb = sb(est, "bk_sb", 128, CCH, F32)
        bo_sb = sb(est, "bo_sb", 128, DCH, F32)
        bb_sb = sb(est, "bb_sb", 128, DCH, F32)
        negC = sb(est, "negC", 128, H, F32)
        negm_r = sb(est, "negm_r", 1, NQC * QCW, F32)
        inv_r = sb(est, "inv_r", 1, NQC * QCW, F32)
        lnscr = sb(est, "lnscr", 1, 3 * QCW, F32)
        ctxOn = ExitStack()
        OnT = ctxOn.enter_context(
            nc.sbuf_tensor("OnT", [128, CCH * TOWN], BF16, side="right"))
        vb_sb = ctxOn.enter_context(
            nc.sbuf_tensor("vb_sb", [128, HE], F32, side="right"))

        sL = Sem(nc, est, "sL")
        sB = Sem(nc, est, "sB")
        sXr = [Sem(nc, est, f"sXr{i}") for i in range(3)]
        sWr = [Sem(nc, est, f"sWr{i}") for i in range(3)]
        sGa = Sem(nc, est, "sGa")
        sXre = [Sem(nc, est, f"sXre{i}") for i in range(6)]
        sWo = [Sem(nc, est, f"sWo{i}") for i in range(2)]
        sFw = [Sem(nc, est, f"sFw{i}") for i in range(8)]
        sOu = [Sem(nc, est, f"sOu{i}") for i in range(8)]
        sG = Sem(nc, est, "sG")
        sO = Sem(nc, est, "sO")
        sCK = Sem(nc, est, "sCK")
        sCV = Sem(nc, est, "sCV")
        sT = Sem(nc, est, "sT")
        sV = Sem(nc, est, "sV")
        sC = Sem(nc, est, "sC")
        sP = Sem(nc, est, "sP")

        block = est.enter_context(nc.Block())

        def _(q):
            for dst, src in [(ident_bf, ident_bf_d), (ident_f, ident_f_d),
                             (ones_row, ones_row_d), (ones_col, ones_col_d),
                             (sel_f, sel_d)]:
                sB.n += 16
                q.dma_start(out=dst[:, :], in_=src[:, :]).then_inc(sB.h, 16)
        block.sync(_)
        n_const = sB.n

        ps0 = ExitStack()
        ps_bias = ps0.enter_context(nc.psum_tensor("ps_bias", [128, 512], F32))
        braw = ps0.enter_context(nc.sbuf_tensor("bias_raw", [max(DCH, CCH), 128], F32))
        bvraw = ps0.enter_context(nc.sbuf_tensor("bv_raw", [1, HE], F32))

        for i, (src, nch, dst) in enumerate(
                [(bq_r, CCH, bq_sb), (bk_r, CCH, bk_sb),
                 (bo_r, DCH, bo_sb), (bb_r, DCH, bb_sb)]):
            vprev = sV.n

            def _(q, src=src, nch=nch, vprev=vprev, i=i):
                if i > 0:
                    q.wait_ge(sV.h, vprev)
                sB.n += 16
                q.dma_start(out=braw[0:nch, :], in_=src[:, :]).then_inc(sB.h, 16)
            block.sync(_)
            lw = sB.n

            def _(te, nch=nch, lw=lw, vprev=vprev, i=i):
                te.wait_ge(sB.h, lw)
                if i != 0:
                    te.wait_ge(sV.h, vprev)
                sT.inc(te.transpose(ps_bias[0:128, 0:nch], braw[0:nch, 0:128],
                                    ident_f[0:nch, 0:nch]))
            block.tensor(_)
            tw = sT.n

            def _(dv, dst=dst, nch=nch, tw=tw):
                dv.wait_ge(sT.h, tw)
                sV.inc(dv.tensor_copy(out=dst[:, 0:nch], in_=ps_bias[:, 0:nch]))
            block.vector(_)

        vprev = sV.n

        def _(q):
            sB.n += 16
            q.dma_start(out=bvraw[:, :], in_=bv_r[:, :]).then_inc(sB.h, 16)
        block.sync(_)
        lw = sB.n

        def _(te, lw=lw, vprev=vprev):
            te.wait_ge(sB.h, lw)
            te.wait_ge(sV.h, vprev)
            sT.inc(te.matmul(ps_bias[:, 0:HE], ones_row[0:1, :], bvraw[0:1, :],
                             start=True, stop=True))
        block.tensor(_)
        tw = sT.n

        def _(dv, tw=tw):
            dv.wait_ge(sT.h, tw)
            sV.inc(dv.tensor_copy(out=vb_sb[:], in_=ps_bias[:, 0:HE]))
        block.vector(_)
        bias_ready_v = sV.n
        ps0.close()

        ctxA = ExitStack()
        xbf = sb(ctxA, "xbf", 128, DCH * TOWN, BF16)
        K_res = sb(ctxA, "K_res", 128, CCH * NKV, BF16)
        V_res = sb(ctxA, "V_res", 128, KCH * VROW, BF16)
        Q_bf = sb(ctxA, "Q_bf", 128, CCH * TOWN, BF16)
        KV_own = sb(ctxA, "KV_own", 128, max(CCH * TOWN, TT * VROW), BF16)
        XR = 3
        xin = [sb(ctxA, f"xin{i}", 128, TOWN, F32) for i in range(XR)]
        WR = 3
        w_f = [sb(ctxA, f"w_f{i}", 128, HE, F32) for i in range(WR)]
        w_b = [sb(ctxA, f"w_b{i}", 128, HE, BF16) for i in range(WR)]
        P_bf = [sb(ctxA, f"P_bf{i}", 128, 2 * QCW, BF16) for i in range(2)]
        zr_t = sb(ctxA, "zr_t", 1, 2 * QCW, F32)
        zrb_sb = sb(ctxA, "zrb_sb", 64, QCW, F32)
        cmax = sb(ctxA, "cmax", 128, 2, F32)
        cmaxb = sb(ctxA, "cmaxb", 128, 1, BF16)

        psA = ExitStack()
        NPROJ = max(CCH * NQC, TT)
        ps_proj = psA.enter_context(nc.psum_tensor("ps_proj", [128, 512 * NPROJ], F32))

        sC_x0 = sC.n

        def _(q):
            for dc in range(DCH):
                if dc >= XR:
                    q.wait_ge(sC.h, sC_x0 + dc - XR + 1)
                sXr[dc % XR].n += 16
                q.dma_start(out=xin[dc % XR][:, :],
                            in_=x_dT[128 * dc:128 * (dc + 1), :]
                            ).then_inc(sXr[dc % XR].h, 16)
        block.sync(_)

        def _(sc):
            for dc in range(DCH):
                sc.wait_ge(sXr[dc % XR].h, 16 * (dc // XR + 1))
                sC.inc(sc.activation(xbf[:, dc * TOWN:(dc + 1) * TOWN],
                                     xin[dc % XR][:, :], AF.Copy))
        block.scalar(_)
        assert sC.n == sC_x0 + DCH
        x_conv = [sC_x0 + dc + 1 for dc in range(DCH)]

        wr_uses = [0, 0, 0]

        def proj_pass(w_dram, ngrp, kind, free_sem, free_cnt, prev_conv_final):
            sWc0 = sP.n
            sT0 = sT.n
            wl = []

            def _(q):
                for dc in range(DCH):
                    if dc >= WR:
                        q.wait_ge(sP.h, sWc0 + dc - WR + 1)
                    elif prev_conv_final is not None:
                        q.wait_ge(sP.h, prev_conv_final)
                    s = dc % WR
                    wr_uses[s] += 1
                    wl.append(16 * wr_uses[s])
                    sWr[s].n += 16
                    q.dma_start(out=w_f[s][:, :],
                                in_=w_dram[128 * dc:128 * (dc + 1), :]
                                ).then_inc(sWr[s].h, 16)
            block.sync(_)

            def _(g):
                for dc in range(DCH):
                    g.wait_ge(sWr[dc % WR].h, wl[dc])
                    if dc >= WR:
                        g.wait_ge(sT.h, sT0 + (dc - WR + 1))
                    sP.inc(g.tensor_copy(out=w_b[dc % WR][:, :], in_=w_f[dc % WR][:, :]))
            block.gpsimd(_)
            assert sP.n == sWc0 + DCH

            grp_cnt = []

            def _(te):
                te.wait_ge(free_sem.h, free_cnt)
                for dc in range(DCH):
                    te.wait_ge(sP.h, sWc0 + dc + 1)
                    te.wait_ge(sC.h, x_conv[dc])
                    last = None
                    for a in range(ngrp):
                        if kind == "ck":
                            cc, qc = divmod(a, NQC)
                            last = te.matmul(
                                ps_proj[:, 512 * a:512 * a + QCW],
                                w_b[dc % WR][:, 128 * cc:128 * (cc + 1)],
                                xbf[:, dc * TOWN + qc * QCW: dc * TOWN + (qc + 1) * QCW],
                                start=(dc == 0), stop=(dc == DCH - 1))
                        else:
                            last = te.matmul(
                                ps_proj[:, 512 * a:512 * a + HE],
                                xbf[:, dc * TOWN + 128 * a: dc * TOWN + 128 * (a + 1)],
                                w_b[dc % WR][:, 0:HE],
                                start=(dc == 0), stop=(dc == DCH - 1))
                        if dc == DCH - 1:
                            grp_cnt.append(sT.inc(last))
                    if dc < DCH - 1:
                        sT.inc(last)
            block.tensor(_)
            assert sT.n == sT0 + (DCH - 1) + ngrp
            return grp_cnt

        kg = proj_pass(wkT, CCH * NQC, "ck", sV, bias_ready_v, None)
        k_conv_final = sP.n

        def _(dv):
            dv.wait_ge(sV.h, bias_ready_v)
            for a in range(CCH * NQC):
                cc, qc = divmod(a, NQC)
                dv.wait_ge(sT.h, kg[a])
                sV.inc(dv.tensor_scalar(
                    KV_own[:, cc * TOWN + qc * QCW: cc * TOWN + (qc + 1) * QCW],
                    ps_proj[:, 512 * a:512 * a + QCW],
                    bk_sb[:, cc:cc + 1], None, OP.add))
        block.vector(_)
        kev_v = sV.n

        def _(g):
            g.wait_ge(sV.h, kev_v)
            for cc in range(CCH):
                sG.n += 16
                g.dma_start(out=ccK_in[128 * cc:128 * (cc + 1), :],
                            in_=KV_own[:, cc * TOWN:(cc + 1) * TOWN]).then_inc(sG.h, 16)
            g.wait_ge(sG.h, sG.n)
            sCK.inc(g.collective_compute(
                "AllGather", OP.bypass,
                replica_groups=[[0, 1, 2, 3], [4, 5, 6, 7]],
                ins=[ccK_in.ap().opt()], outs=[ccK_out.ap().opt()]))
        block.gpsimd(_)

        vg = proj_pass(wvT, TT, "v", sV, kev_v, k_conv_final)
        v_conv_final = sP.n

        ccK_dma_g = sG.n

        def _(dv):
            dv.wait_ge(sV.h, bias_ready_v)
            dv.wait_ge(sG.h, ccK_dma_g)
            for tt in range(TT):
                dv.wait_ge(sT.h, vg[tt])
                last = None
                for h in range(H):
                    last = dv.tensor_add(
                        KV_own[:, tt * VROW + VW * h: tt * VROW + VW * h + E],
                        ps_proj[:, 512 * tt + E * h: 512 * tt + E * (h + 1)],
                        vb_sb[:, E * h:E * (h + 1)])
                sV.inc(last)
        block.vector(_)
        sV_V = sV.n

        def _(g):
            g.wait_ge(sV.h, sV_V)
            last = None
            for tt in range(TT):
                for h in range(H):
                    last = g.memset(
                        KV_own[:, tt * VROW + VW * h + E: tt * VROW + VW * (h + 1)], 1.0)
            sP.inc(last)
            for tt in range(TT):
                sG.n += 16
                g.dma_start(out=ccV_in[128 * tt:128 * (tt + 1), :],
                            in_=KV_own[:, tt * VROW:(tt + 1) * VROW]).then_inc(sG.h, 16)
            g.wait_ge(sG.h, sG.n)
            sCV.inc(g.collective_compute(
                "AllGather", OP.bypass,
                replica_groups=[[0, 1, 2, 3], [4, 5, 6, 7]],
                ins=[ccV_in.ap().opt()], outs=[ccV_out.ap().opt()]))
        block.gpsimd(_)

        qg = proj_pass(wqT, CCH * NQC, "ck", sV, sV_V, v_conv_final)

        def _(dv):
            for a in range(CCH * NQC):
                cc, qc = divmod(a, NQC)
                dv.wait_ge(sT.h, qg[a])
                sV.inc(dv.tensor_scalar(
                    Q_bf[:, cc * TOWN + qc * QCW: cc * TOWN + (qc + 1) * QCW],
                    ps_proj[:, 512 * a:512 * a + QCW],
                    bq_sb[:, cc:cc + 1], None, OP.add))
        block.vector(_)
        qev_v = sV.n

        def _(q):
            q.wait_ge(sCK.h, 1)
            for cc in range(CCH):
                for r in range(GRP):
                    sGa.n += 16
                    q.dma_start(
                        out=K_res[:, cc * NKV + TOWN * r: cc * NKV + TOWN * (r + 1)],
                        in_=ccK_out[HE * r + 128 * cc: HE * r + 128 * (cc + 1), :]
                    ).then_inc(sGa.h, 16)
            q.wait_ge(sCV.h, 1)
            for kt in range(KCH):
                sGa.n += 16
                q.dma_start(out=V_res[:, kt * VROW:(kt + 1) * VROW],
                            in_=ccV_out[128 * kt:128 * (kt + 1), :]).then_inc(sGa.h, 16)
        block.sync(_)
        sL_gather = sGa.n

        if stop_after == "proj":
            psA.close()
            ctxA.close()
            return nc

        if debug_dumps:
            def _(g):
                g.wait_ge(sGa.h, sL_gather)
                g.wait_ge(sV.h, qev_v)
                for dst, srct in [(dbg_K, K_res), (dbg_V, V_res), (dbg_Q, Q_bf)]:
                    sG.n += 16
                    g.dma_start(out=dst[:, :], in_=srct[:, :]).then_inc(sG.h, 16)
                g.wait_ge(sG.h, sG.n)
            block.gpsimd(_)

        psA.close()
        psA2 = ExitStack()
        ps_S = [psA2.enter_context(nc.psum_tensor(f"ps_S{i}", [128, 2 * QCW], F32))
                for i in range(2)]
        ps_PV = [psA2.enter_context(nc.psum_tensor(f"ps_PV{i}", [128, 512], F32))
                 for i in range(2)]
        ps_misc = psA2.enter_context(nc.psum_tensor("ps_misc", [128, 512], F32))

        last_exp_psS = [0, 0]
        last_pv_Pbf = [sT.n, sT.n]
        evac_done = {-2: sV.n, -1: sV.n}
        negc_ready = {}
        misc_free = sV.n
        last_cmT = [sT.n]
        first_att = True

        for qc in range(NQC):
            for h in range(H):
                idx = qc * H + h
                hb = idx % 2
                cc = h // 2
                rb = 64 * (h % 2)
                need_c = qc == 0
                pv_cnt = {}
                exp_cnt = {}
                s_cnt = {}

                def q_rhs(qc=qc, cc=cc, rb=rb):
                    return Q_bf[rb:rb + E,
                                cc * TOWN + qc * QCW: cc * TOWN + (qc + 1) * QCW]

                def s_lhs(k, cc=cc, rb=rb):
                    return K_res[rb:rb + E,
                                 cc * NKV + 128 * k: cc * NKV + 128 * (k + 1)]

                def pv_step(te, gp, h=h, hb=hb):
                    last = None
                    for j in range(2):
                        k = 2 * gp + j
                        last = te.matmul(
                            ps_PV[hb][0:VW, 0:QCW],
                            V_res[:, k * VROW + VW * h: k * VROW + VW * (h + 1)],
                            P_bf[gp % 2][:, j * QCW:(j + 1) * QCW],
                            start=(k == 0), stop=(k == KCH - 1))
                    pv_cnt[gp] = sT.inc(last)
                    last_pv_Pbf[gp % 2] = pv_cnt[gp]

                for g in range(NG):
                    par = g % 2
                    fa = first_att
                    first_att = False

                    def _(te, g=g, par=par, fa=fa, idx=idx):
                        if fa:
                            te.wait_ge(sGa.h, sL_gather)
                            te.wait_ge(sV.h, qev_v)
                        te.wait_ge(sC.h, last_exp_psS[par])
                        last = None
                        for j in range(2):
                            last = te.matmul(
                                ps_S[par][:, j * QCW:(j + 1) * QCW],
                                s_lhs(2 * g + j), q_rhs(), start=True, stop=True)
                        s_cnt[g] = sT.inc(last)
                        if g >= 1:
                            gp = g - 1
                            te.wait_ge(sC.h, exp_cnt[gp])
                            if gp == 0:
                                te.wait_ge(sV.h, evac_done[idx - 2])
                            pv_step(te, gp)
                    block.tensor(_)

                    if need_c and g == 0:
                        def _(dv, lcm=last_cmT[0]):
                            dv.wait_ge(sT.h, s_cnt[0])
                            dv.wait_ge(sT.h, lcm)
                            dv.reduce_max(cmax[:, 0:1], ps_S[0][:, 0:2 * QCW],
                                          axis=AX.X)
                            dv.engine_nop()
                            dv.engine_nop()
                            sV.inc(dv.tensor_copy(out=cmaxb[:, 0:1],
                                                  in_=cmax[:, 0:1]))
                        block.vector(_)
                        vv = sV.n

                        def _(te, vv=vv, mf=misc_free):
                            te.wait_ge(sV.h, vv)
                            te.wait_ge(sV.h, mf)
                            sT.inc(te.matmul(ps_misc[0:1, 128:256], cmaxb[:, 0:1],
                                             ident_bf[:, :], start=True, stop=True))
                        block.tensor(_)
                        tv = sT.n
                        last_cmT[0] = tv

                        def _(dv, tv=tv):
                            dv.wait_ge(sT.h, tv)
                            dv.reduce_max(cmax[0:1, 1:2], ps_misc[0:1, 128:256],
                                          axis=AX.X, negate=True)
                            dv.engine_nop()
                            sV.inc(dv.engine_nop())
                        block.vector(_)
                        vv2 = sV.n

                        def _(te, vv2=vv2):
                            te.wait_ge(sV.h, vv2)
                            sT.inc(te.matmul(ps_misc[:, 0:1], ones_row[0:1, :],
                                             cmax[0:1, 1:2], start=True, stop=True))
                        block.tensor(_)
                        tv2 = sT.n

                        def _(dv, h=h, tv2=tv2):
                            dv.wait_ge(sT.h, tv2)
                            sV.inc(dv.tensor_copy(out=negC[:, h:h + 1],
                                                  in_=ps_misc[:, 0:1]))
                        block.vector(_)
                        negc_ready[h] = sV.n
                        misc_free = sV.n

                    def _(sc, g=g, par=par, h=h):
                        sc.wait_ge(sT.h, s_cnt[g])
                        if g == 0:
                            sc.wait_ge(sV.h, negc_ready[h])
                        sc.wait_ge(sT.h, last_pv_Pbf[par])
                        exp_cnt[g] = sC.inc(sc.activation(
                            P_bf[par][:, 0:2 * QCW], ps_S[par][:, 0:2 * QCW],
                            AF.Exp, bias=negC[:, h:h + 1]))
                        last_exp_psS[par] = exp_cnt[g]
                    block.scalar(_)

                def _(te, idx=idx):
                    gp = NG - 1
                    te.wait_ge(sC.h, exp_cnt[gp])
                    if gp == 0:
                        te.wait_ge(sV.h, evac_done[idx - 2])
                    pv_step(te, gp)
                block.tensor(_)

                def _(dv, hb=hb):
                    dv.wait_ge(sT.h, pv_cnt[NG - 1])
                    dv.tensor_scalar(zr_t[0:1, 0:QCW], ps_PV[hb][64:65, 0:QCW],
                                     float(math.sqrt(HEAD_DIM)), 1e-30,
                                     OP.mult, OP.add)
                    dv.engine_nop()
                    dv.engine_nop()
                    dv.reciprocal(zr_t[0:1, QCW:2 * QCW], zr_t[0:1, 0:QCW])
                    dv.engine_nop()
                    sV.inc(dv.engine_nop())
                block.vector(_)
                vv3 = sV.n

                def _(te, vv3=vv3, mf=misc_free):
                    te.wait_ge(sV.h, vv3)
                    te.wait_ge(sV.h, mf)
                    sT.inc(te.matmul(ps_misc[0:64, 0:QCW], ones_row[0:1, 0:64],
                                     zr_t[0:1, QCW:2 * QCW], start=True, stop=True))
                block.tensor(_)
                tv3 = sT.n

                def _(dv, tv3=tv3, hb=hb, h=h, cc=cc, rb=rb, qc=qc):
                    dv.wait_ge(sT.h, tv3)
                    dv.tensor_copy(out=zrb_sb[:, 0:QCW], in_=ps_misc[0:64, 0:QCW])
                    dv.engine_nop()
                    dv.engine_nop()
                    sV.inc(dv.tensor_mul(
                        OnT[rb:rb + 64,
                            cc * TOWN + qc * QCW: cc * TOWN + (qc + 1) * QCW],
                        ps_PV[hb][0:64, 0:QCW], zrb_sb[:, 0:QCW]))
                block.vector(_)
                evac_done[idx] = sV.n
                misc_free = sV.n

        att_done_v = sV.n
        if debug_dumps:
            def _(g):
                g.wait_ge(sV.h, att_done_v)
                sG.n += 16
                g.dma_start(out=dbg_OnT[:, :], in_=OnT[:, :]).then_inc(sG.h, 16)
                sG.n += 16
                g.dma_start(out=dbg_negC[:, :], in_=negC[:, :]).then_inc(sG.h, 16)
                g.wait_ge(sG.h, sG.n)
            block.gpsimd(_)
        psA2.close()
        if stop_after == "attn":
            ctxA.close()
            return nc
        ctxA.close()

        ctxB = ExitStack()
        R = sb(ctxB, "R", 128, DCH * TOWN, BF16)
        NMb = [sb(ctxB, f"NMb{i}", 128, QCW, F32) for i in range(NQC)]
        Ib = [sb(ctxB, f"Ib{i}", 128, QCW, F32) for i in range(NQC)]

        ctxB1 = ExitStack()
        WOS = 2
        wo_f = [sb(ctxB1, f"wo_f{i}", 128, 128 * OGW, F32) for i in range(WOS * CCH)]
        wo_b = [sb(ctxB1, f"wo_b{i}", 128, 128 * OGW, BF16) for i in range(WOS * CCH)]
        XRN = 6
        xre = [sb(ctxB1, f"xre{i}", 128, QCW, F32) for i in range(XRN)]
        scrA = [sb(ctxB1, f"scrA{i}", 128, QCW, F32) for i in range(4)]
        y1scr = [sb(ctxB1, f"y1scr{i}", 128, QCW, BF16) for i in range(8)]

        psB1 = ExitStack()
        ps_W = [psB1.enter_context(nc.psum_tensor(f"ps_W{i}", [128, 512], F32))
                for i in range(4)]
        ps_s1 = [psB1.enter_context(nc.psum_tensor(f"ps_s1_{i}", [128, 512], F32))
                 for i in range(NQC)]
        ps_s2 = [psB1.enter_context(nc.psum_tensor(f"ps_s2_{i}", [128, 512], F32))
                 for i in range(NQC)]

        NOG = DCH // OGW
        wo_conv = {}
        wo_mm_done = {}
        scr_sc = {}
        scr_free = {-i: sV.n for i in range(1, 5)}
        y1_free = {-i: sT.n for i in range(1, 9)}
        sq_pw = {}
        SLAG = 3

        def emit_stats(oi2):
            o2_, qc2 = divmod(oi2, NQC)

            def _(te, o=o2_, qc=qc2, oi2=oi2):
                te.wait_ge(sP.h, sq_pw[oi2])
                sl = y1scr[oi2 % 8]
                te.matmul(ps_s1[qc][0:1, 0:QCW], ones_col[:, 0:1],
                          R[:, o * TOWN + qc * QCW: o * TOWN + (qc + 1) * QCW],
                          start=(o == 0), stop=(o == DCH - 1))
                last = te.matmul(ps_s2[qc][0:1, 0:QCW], ones_col[:, 0:1],
                                 sl[:, 0:QCW],
                                 start=(o == 0), stop=(o == DCH - 1))
                y1_free[oi2] = sT.inc(last)
            block.tensor(_)
        xre_free = {-i: sV.n for i in range(1, XRN + 1)}
        oi = 0
        for og in range(NOG):
            def _(q, og=og):
                if og == 0:
                    q.wait_ge(sV.h, att_done_v)
                if og >= WOS:
                    q.wait_ge(sP.h, wo_conv[og - WOS])
                for ccx in range(CCH):
                    sWo[og % WOS].n += 16
                    q.dma_start(
                        out=wo_f[(og % WOS) * CCH + ccx][:, :],
                        in_=woT[128 * ccx:128 * (ccx + 1), 128 * OGW * og:128 * OGW * (og + 1)]
                    ).then_inc(sWo[og % WOS].h, 16)
            block.sync(_)

            def _(dv, og=og):
                dv.wait_ge(sWo[og % WOS].h, 16 * CCH * (og // WOS + 1))
                if og >= WOS:
                    dv.wait_ge(sT.h, wo_mm_done[og - WOS])
                last = None
                for ccx in range(CCH):
                    last = dv.tensor_copy(out=wo_b[(og % WOS) * CCH + ccx][:, :],
                                          in_=wo_f[(og % WOS) * CCH + ccx][:, :])
                wo_conv[og] = sV.inc(last)
            block.vector(_)

            for o4 in range(OGW):
                o = og * OGW + o4
                for qc in range(NQC):
                    par = oi % 4

                    def _(te, og=og, o4=o4, o=o, qc=qc, par=par, oi=oi):
                        te.wait_ge(sV.h, wo_conv[og])
                        if oi >= 4:
                            te.wait_ge(sV.h, scr_sc[oi - 4])
                        last = None
                        for ccx in range(CCH):
                            last = te.matmul(
                                ps_W[par][:, 0:QCW],
                                wo_b[(og % WOS) * CCH + ccx][:, 128 * o4:128 * (o4 + 1)],
                                OnT[:, ccx * TOWN + qc * QCW: ccx * TOWN + (qc + 1) * QCW],
                                start=(ccx == 0), stop=(ccx == CCH - 1))
                        sT.inc(last)
                    block.tensor(_)
                    tw = sT.n

                    def _(q, o=o, qc=qc, oi=oi):
                        q.wait_ge(sV.h, xre_free[oi - XRN])
                        sXre[oi % XRN].n += 16
                        q.dma_start(
                            out=xre[oi % XRN][:, :],
                            in_=x_dT[128 * o:128 * (o + 1), qc * QCW:(qc + 1) * QCW]
                        ).then_inc(sXre[oi % XRN].h, 16)
                    block.sync(_)
                    lw = sXre[oi % XRN].n

                    def _(dv, o=o, qc=qc, par=par, oi=oi, lw=lw, tw=tw):
                        dv.wait_ge(sT.h, tw)
                        if oi >= 4:
                            dv.wait_ge(sV.h, scr_free[oi - 4])
                        dv.tensor_scalar(
                            scrA[oi % 4][:, :], ps_W[par][:, 0:QCW],
                            bo_sb[:, o:o + 1], None, OP.add)
                        dv.wait_ge(sXre[oi % XRN].h, lw)
                        sV.inc(dv.tensor_add(
                            R[:, o * TOWN + qc * QCW: o * TOWN + (qc + 1) * QCW],
                            scrA[oi % 4][:, :], xre[oi % XRN][:, :]))
                    block.vector(_)
                    scr_free[oi] = sV.n
                    scr_sc[oi] = sV.n
                    xre_free[oi] = sV.n
                    vw = sV.n

                    def _(g, o=o, qc=qc, oi=oi, vw=vw):
                        g.wait_ge(sV.h, vw)
                        if oi >= 8:
                            g.wait_ge(sT.h, y1_free[oi - 8])
                        sl = y1scr[oi % 8]
                        sP.inc(g.tensor_mul(
                            sl[:, 0:QCW],
                            R[:, o * TOWN + qc * QCW: o * TOWN + (qc + 1) * QCW],
                            R[:, o * TOWN + qc * QCW: o * TOWN + (qc + 1) * QCW]))
                    block.gpsimd(_)
                    sq_pw[oi] = sP.n

                    if oi >= SLAG:
                        emit_stats(oi - SLAG)
                    oi += 1
            wo_mm_done[og] = sT.n
        for oi2 in range(max(0, oi - SLAG), oi):
            emit_stats(oi2)

        stat_t = sT.n
        for qc in range(NQC):
            prev_sc = sC.n

            def _(dv, qc=qc, prev_sc=prev_sc):
                dv.wait_ge(sT.h, stat_t)
                dv.wait_ge(sC.h, prev_sc)
                nm = negm_r[0:1, qc * QCW:(qc + 1) * QCW]
                dv.tensor_scalar(nm, ps_s1[qc][0:1, 0:QCW], -1.0 / D, None, OP.mult)
                dv.tensor_scalar(lnscr[0:1, 0:QCW], ps_s2[qc][0:1, 0:QCW],
                                 1.0 / D, None, OP.mult)
                dv.engine_nop()
                dv.engine_nop()
                dv.tensor_mul(lnscr[0:1, QCW:2 * QCW], nm, nm)
                sV.inc(dv.scalar_tensor_tensor(lnscr[0:1, 0:QCW], lnscr[0:1, 0:QCW],
                                               1e-5, lnscr[0:1, QCW:2 * QCW],
                                               OP.add, OP.subtract))
            block.vector(_)
            vw = sV.n

            def _(sc, qc=qc, vw=vw):
                sc.wait_ge(sV.h, vw)
                sc.activation(lnscr[0:1, 0:QCW], lnscr[0:1, 0:QCW], AF.Ln)
                sC.inc(sc.activation(inv_r[0:1, qc * QCW:(qc + 1) * QCW],
                                     lnscr[0:1, 0:QCW], AF.Exp, scale=-0.5))
            block.scalar(_)
        ln1_c = sC.n

        if debug_dumps:
            def _(g, c0=ln1_c):
                g.wait_ge(sC.h, c0)
                sG.n += 16
                g.dma_start(out=dbg_R[:, :], in_=R[:, :]).then_inc(sG.h, 16)
                sG.n += 16
                g.dma_start(out=dbg_rows[0:1, 0:NQC * QCW],
                            in_=negm_r[0:1, :]).then_inc(sG.h, 16)
                sG.n += 16
                g.dma_start(out=dbg_rows[0:1, NQC * QCW:2 * NQC * QCW],
                            in_=inv_r[0:1, :]).then_inc(sG.h, 16)
                g.wait_ge(sG.h, sG.n)
            block.gpsimd(_)
        psB1.close()
        ctxB1.close()
        ctxOn.close()

        ctxB2 = ExitStack()
        x1f = sb(ctxB2, "x1f", 128, DCH * TOWN, BF16)
        gw_bf = sb(ctxB2, "gw_bf", 128, NE * DCH, BF16)
        FW = len(sFw)
        fw_f = [sb(ctxB2, f"fw_f{i}", 128, 256, F32) for i in range(FW)]
        fw_b = [sb(ctxB2, f"fw_b{i}", 128, 256, BF16) for i in range(FW)]
        gscr = [sb(ctxB2, f"gscr{i}", 128, QCW, F32) for i in range(2)]
        LW = sb(ctxB2, "LW", 128, 512, F32)
        GA = sb(ctxB2, "GA", 128, 512, F32)
        GB = sb(ctxB2, "GB", 128, 512, F32)
        LW2 = sb(ctxB2, "LW2", 128, 512, F32)
        GA2 = sb(ctxB2, "GA2", 128, 512, F32)
        GB2 = sb(ctxB2, "GB2", 128, 512, F32)
        gu_sb = sb(ctxB2, "gu_sb", NE, QCW, F32)
        wch_sb = sb(ctxB2, "wch_sb", NE, QCW, F32)
        wb_sb = [sb(ctxB2, f"wb_sb{i}", 128, QCW, F32) for i in range(NQC)]
        low_bf = [sb(ctxB2, f"low_bf{i}", 128, QCW, BF16) for i in range(NQC)]
        wb3 = sb(ctxB2, "wb3", 128, NE, BF16)
        wb3b = sb(ctxB2, "wb3b", 128, NE, BF16)
        tcol = sb(ctxB2, "tcol", 128, 16, F32)
        outst = ([gscr[0], gscr[1], LW, GA, GB]
                 + ([Ib[1], wb_sb[0], wb_sb[1]] if NQC == 2 else [wb_sb[0]]))
        NOUT = len(outst)
        evsc = [gscr[0], gscr[1], LW, GA]
        NM2b, I2b = NMb[0], Ib[0]

        psB2a = ExitStack()
        ps_m1 = psB2a.enter_context(nc.psum_tensor("ps_m1", [128, 512], F32))
        ps_m2 = psB2a.enter_context(nc.psum_tensor("ps_m2", [128, 512], F32))

        b1_t, b1_v, b1_c, b1_p = sT.n, sV.n, sC.n, sP.n

        fw_uses = [0] * FW

        def _(q):
            q.wait_ge(sT.h, b1_t)
            q.wait_ge(sV.h, b1_v)
            q.wait_ge(sC.h, b1_c)
            q.wait_ge(sP.h, b1_p)
            fw_uses[0] += 1
            sFw[0].n += 16
            q.dma_start(out=fw_f[0][:, 0:NE * DCH],
                        in_=gw_pk[:, :]).then_inc(sFw[0].h, 16)
        block.sync(_)
        lw = sFw[0].n

        def _(g, lw=lw):
            g.wait_ge(sFw[0].h, lw)
            sP.inc(g.tensor_copy(out=gw_bf[:, :], in_=fw_f[0][:, 0:NE * DCH]))
        block.gpsimd(_)
        gw_p = sP.n

        fst = {"j": 0, "conv": {}, "cons": {}, "load": {}}

        def cwait(eng, j):
            pe, pc = fst["conv"][j]
            eng.wait_ge(sV.h if pe == "v" else sP.h, pc)

        def fw_item(src_ap, width, eng="g"):
            j = fst["j"]
            slot = j % FW

            def _(q, j=j, slot=slot, src_ap=src_ap, width=width):
                if j >= FW:
                    cwait(q, j - FW)
                else:
                    q.wait_ge(sP.h, gw_p)
                fw_uses[slot] += 1
                fst["load"][j] = 16 * fw_uses[slot]
                sFw[slot].n += 16
                q.dma_start(out=fw_f[slot][:, 0:width],
                            in_=src_ap).then_inc(sFw[slot].h, 16)
            block.sync(_)

            if eng == "g":
                def _(g, j=j, slot=slot, width=width):
                    g.wait_ge(sFw[slot].h, fst["load"][j])
                    if j >= FW:
                        g.wait_ge(sT.h, fst["cons"][j - FW])
                    fst["conv"][j] = ("g", sP.inc(g.tensor_copy(
                        out=fw_b[slot][:, 0:width], in_=fw_f[slot][:, 0:width])))
                block.gpsimd(_)
            else:
                def _(dv, j=j, slot=slot, width=width):
                    dv.wait_ge(sFw[slot].h, fst["load"][j])
                    if j >= FW:
                        dv.wait_ge(sT.h, fst["cons"][j - FW])
                    fst["conv"][j] = ("v", sV.inc(dv.tensor_copy(
                        out=fw_b[slot][:, 0:width], in_=fw_f[slot][:, 0:width])))
                block.vector(_)
            fst["j"] = j + 1
            return j, slot

        def fst_done(j, cnt):
            fst["cons"][j] = cnt

        bc_v = [None] * NQC
        for qcx in range(NQC):
            def _(te, qcx=qcx):
                te.wait_ge(sC.h, ln1_c)
                if qcx > 0:
                    te.wait_ge(sV.h, bc_v[qcx - 1])
                te.matmul(ps_m1[:, 0:QCW], ones_row[0:1, :],
                          negm_r[0:1, qcx * QCW:(qcx + 1) * QCW],
                          start=True, stop=True)
                sT.inc(te.matmul(ps_m2[:, 0:QCW], ones_row[0:1, :],
                                 inv_r[0:1, qcx * QCW:(qcx + 1) * QCW],
                                 start=True, stop=True))
            block.tensor(_)
            tw = sT.n

            def _(dv, qcx=qcx, tw=tw):
                dv.wait_ge(sT.h, tw)
                dv.tensor_copy(out=NMb[qcx][:, :], in_=ps_m1[:, 0:QCW])
                dv.tensor_copy(out=Ib[qcx][:, :], in_=ps_m2[:, 0:QCW])
                dv.engine_nop()
                bc_v[qcx] = sV.inc(dv.engine_nop())
            block.vector(_)

        x1c = [[None] * DCH for _ in range(NQC)]

        def xwait(eng, qcx, dc):
            pe, pc = x1c[qcx][dc]
            eng.wait_ge(sV.h if pe == "v" else sP.h, pc)

        for qcx in range(NQC):
            gp_dcs = [dc for dc in range(DCH) if dc % 4 == 3]
            for dc in [d for d in range(DCH) if d % 4 != 3]:
                def _(dv, qcx=qcx, dc=dc):
                    ch = slice(dc * TOWN + qcx * QCW, dc * TOWN + (qcx + 1) * QCW)
                    dv.tensor_add(gscr[0][:, :], R[:, ch], NMb[qcx][:, :])
                    x1c[qcx][dc] = ("v", sV.inc(dv.tensor_mul(
                        x1f[:, ch], gscr[0][:, :], Ib[qcx][:, :])))
                block.vector(_)
            for i, dc in enumerate(gp_dcs):
                def _(g, qcx=qcx, dc=dc, i=i):
                    if i == 0:
                        g.wait_ge(sV.h, bc_v[qcx])
                    ch = slice(dc * TOWN + qcx * QCW, dc * TOWN + (qcx + 1) * QCW)
                    g.tensor_add(gscr[1][:, :], R[:, ch], NMb[qcx][:, :])
                    x1c[qcx][dc] = ("g", sP.inc(g.tensor_mul(
                        x1f[:, ch], gscr[1][:, :], Ib[qcx][:, :])))
                block.gpsimd(_)
        if debug_dumps:
            def _(g):
                xwait(g, NQC - 1, DCH - 1)
                xwait(g, NQC - 1, DCH - 2)
                sG.n += 16
                g.dma_start(out=dbg_x1q[:, :], in_=x1f[:, :]).then_inc(sG.h, 16)
                g.wait_ge(sG.h, sG.n)
            block.gpsimd(_)

        NT4 = QCW // 128
        wbv = [None] * NQC
        for qcx in range(NQC):
            def _(te, qcx=qcx):
                te.wait_ge(sP.h, gw_p)
                if qcx > 0:
                    te.wait_ge(sV.h, wbv[qcx - 1])
                last = None
                for dc in range(DCH):
                    xwait(te, qcx, dc)
                    last = te.matmul(ps_m1[0:NE, 0:QCW],
                                     gw_bf[:, NE * dc:NE * (dc + 1)],
                                     x1f[:, dc * TOWN + qcx * QCW:
                                         dc * TOWN + (qcx + 1) * QCW],
                                     start=(dc == 0), stop=(dc == DCH - 1))
                sT.inc(last)
            block.tensor(_)
            tw = sT.n

            def _(dv, tw=tw):
                dv.wait_ge(sT.h, tw)
                sV.inc(dv.tensor_copy(out=gu_sb[:, :], in_=ps_m1[0:NE, 0:QCW]))
            block.vector(_)
            vw = sV.n

            def _(te, vw=vw):
                te.wait_ge(sV.h, vw)
                last = None
                for t4 in range(NT4):
                    last = te.matmul(ps_m2[:, 8 * t4:8 * (t4 + 1)],
                                     gu_sb[0:NE, 128 * t4:128 * (t4 + 1)],
                                     ident_f[0:NE, 0:NE], start=True, stop=True)
                sT.inc(last)
            block.tensor(_)
            tw2 = sT.n

            if debug_dumps and qcx == NQC - 1:
                def _(g, tw2=tw2):
                    g.wait_ge(sT.h, tw2)
                    sG.n += 16
                    g.dma_start(out=dbg_gu[:, :], in_=gu_sb[:, :]).then_inc(sG.h, 16)
                    g.wait_ge(sG.h, sG.n)
                block.gpsimd(_)

            LWs, GAs, GBs, wb3s = [LW, LW2], [GA, GA2], [GB, GB2], [wb3, wb3b]
            wb3_free = {0: 0, 1: 0}
            sig_cw = {}

            def t4_part1(t4, tw2):
                p = t4 % 2
                tb = 8 * p

                def _(dv, t4=t4, tw2=tw2, p=p, tb=tb, lwb=wb3_free[p]):
                    dv.wait_ge(sT.h, tw2)
                    dv.wait_ge(sT.h, lwb)
                    dv.tensor_copy(out=LWs[p][:, 0:NE],
                                   in_=ps_m2[:, 8 * t4:8 * (t4 + 1)])
                    dv.memset(LWs[p][:, NE:512], -3e38)
                    dv.engine_nop()
                    dv.engine_nop()
                    dv.reduce_max(tcol[:, tb + 0:tb + 1], LWs[p][:, :], axis=AX.X)
                    dv.engine_nop()
                    dv.engine_nop()
                    dv.tensor_scalar(GAs[p][:, :], LWs[p][:, :],
                                     tcol[:, tb + 0:tb + 1], None, OP.is_equal)
                    dv.scalar_tensor_tensor(GBs[p][:, :], GAs[p][:, :],
                                            -1e30, LWs[p][:, :], OP.mult, OP.add)
                    dv.engine_nop()
                    dv.engine_nop()
                    dv.reduce_max(tcol[:, tb + 2:tb + 3], GBs[p][:, :], axis=AX.X)
                    dv.engine_nop()
                    dv.engine_nop()
                    dv.tensor_scalar(GBs[p][:, :], LWs[p][:, :],
                                     tcol[:, tb + 2:tb + 3], None, OP.is_ge)
                    dv.tensor_sub(GBs[p][:, :], GBs[p][:, :], GAs[p][:, :])
                    dv.tensor_sub(tcol[:, tb + 3:tb + 4], tcol[:, tb + 0:tb + 1],
                                  tcol[:, tb + 2:tb + 3])
                    sV.inc(dv.tensor_sub(tcol[:, tb + 4:tb + 5],
                                         tcol[:, tb + 2:tb + 3],
                                         tcol[:, tb + 0:tb + 1]))
                block.vector(_)
                vw2 = sV.n

                def _(sc, tb=tb, vw2=vw2):
                    sc.wait_ge(sV.h, vw2)
                    sc.activation(tcol[:, tb + 5:tb + 6], tcol[:, tb + 3:tb + 4],
                                  AF.Sigmoid)
                    sig_cw[t4] = sC.inc(sc.activation(
                        tcol[:, tb + 6:tb + 7], tcol[:, tb + 4:tb + 5],
                        AF.Sigmoid))
                block.scalar(_)

            def t4_part2(t4):
                p = t4 % 2
                tb = 8 * p

                def _(dv, t4=t4, p=p, tb=tb):
                    dv.wait_ge(sC.h, sig_cw[t4])
                    dv.tensor_scalar(GAs[p][:, :], GAs[p][:, :],
                                     tcol[:, tb + 5:tb + 6], None, OP.mult)
                    dv.tensor_scalar(GBs[p][:, :], GBs[p][:, :],
                                     tcol[:, tb + 6:tb + 7], None, OP.mult)
                    dv.tensor_add(GAs[p][:, :], GAs[p][:, :], GBs[p][:, :])
                    sV.inc(dv.tensor_copy(out=wb3s[p][:, :], in_=GAs[p][:, 0:NE]))
                block.vector(_)
                vw3 = sV.n

                def _(te, t4=t4, p=p, vw3=vw3):
                    te.wait_ge(sV.h, vw3)
                    sT.inc(te.matmul(ps_m1[0:NE, 128 * t4:128 * (t4 + 1)],
                                     wb3s[p][:, :], ident_bf[:, :],
                                     start=True, stop=True))
                block.tensor(_)
                wb3_free[p] = sT.n

            for t4 in range(NT4):
                t4_part1(t4, tw2)
                if t4 >= 1:
                    t4_part2(t4 - 1)
            t4_part2(NT4 - 1)
            tw3 = sT.n

            def _(dv, tw3=tw3):
                dv.wait_ge(sT.h, tw3)
                sV.inc(dv.tensor_copy(out=wch_sb[:, :], in_=ps_m1[0:NE, 0:QCW]))
            block.vector(_)
            vw4 = sV.n

            def _(te, vw4=vw4):
                te.wait_ge(sV.h, vw4)
                sT.inc(te.matmul(ps_m2[:, 0:QCW], sel_f[:, :], wch_sb[:, :],
                                 start=True, stop=True))
            block.tensor(_)
            tw4 = sT.n

            def _(dv, tw4=tw4, qcx=qcx):
                dv.wait_ge(sT.h, tw4)
                wbv[qcx] = sV.inc(dv.tensor_copy(out=wb_sb[qcx][:, :],
                                                 in_=ps_m2[:, 0:QCW]))
            block.vector(_)
            if debug_dumps and qcx == NQC - 1:
                def _(g, qcx=qcx):
                    g.wait_ge(sV.h, wbv[qcx])
                    sG.n += 16
                    g.dma_start(out=dbg_wch[:, :], in_=wch_sb[:, :]).then_inc(sG.h, 16)
                    g.wait_ge(sG.h, sG.n)
                block.gpsimd(_)

        for dc in range(DCH):
            j, slot = fw_item(la_pk[:, 128 * dc:128 * (dc + 1)], 128)

            def _(te, dc=dc, j=j, slot=slot):
                cwait(te, j)
                if dc == 0:
                    te.wait_ge(sV.h, wbv[NQC - 1])
                last = None
                for qcx in range(NQC):
                    pb = ps_m1 if qcx == 0 else ps_m2
                    last = te.matmul(pb[:, 0:QCW], fw_b[slot][:, 0:128],
                                     x1f[:, dc * TOWN + qcx * QCW:
                                         dc * TOWN + (qcx + 1) * QCW],
                                     start=(dc == 0), stop=(dc == DCH - 1))
                fst_done(j, sT.inc(last))
            block.tensor(_)
        tw5 = sT.n

        def _(dv, tw5=tw5):
            dv.wait_ge(sT.h, tw5)
            for qcx in range(NQC):
                pb = ps_m1 if qcx == 0 else ps_m2
                sV.inc(dv.tensor_mul(low_bf[qcx][:, :], pb[:, 0:QCW],
                                     wb_sb[qcx][:, :]))
        block.vector(_)
        low_v = sV.n
        if debug_dumps:
            def _(g):
                g.wait_ge(sV.h, low_v)
                sG.n += 16
                g.dma_start(out=dbg_low[:, :], in_=low_bf[0][:, :]).then_inc(sG.h, 16)
                g.wait_ge(sG.h, sG.n)
            block.gpsimd(_)

        psB2a.close()
        psB2b = ExitStack()
        NB = 2 * NQC
        ps_B = [psB2b.enter_context(nc.psum_tensor(f"ps_B{i}", [128, 512], F32))
                for i in range(2 * NB)]

        bw_done = {}
        ev_v = {-2: sV.n, -1: sV.n}
        NOB = DCH // 2

        def emit_evac(ob):
            def _(dv, ob=ob):
                dv.wait_ge(sT.h, bw_done[ob])
                for o2 in range(2):
                    for qcx in range(NQC):
                        ei = o2 * NQC + qcx
                        dv.tensor_scalar(evsc[ei][:, 0:QCW],
                                         ps_B[(ob % 2) * NB + ei][:, 0:QCW],
                                         bb_sb[:, ob * 2 + o2:ob * 2 + o2 + 1],
                                         None, OP.add)
                for o2 in range(2):
                    for qcx in range(NQC):
                        ei = o2 * NQC + qcx
                        o = ob * 2 + o2
                        ch = slice(o * TOWN + qcx * QCW,
                                   o * TOWN + (qcx + 1) * QCW)
                        sV.inc(dv.tensor_add(R[:, ch], x1f[:, ch],
                                             evsc[ei][:, 0:QCW]))
            block.vector(_)
            ev_v[ob] = sV.n

        for ob in range(NOB):
            for dc in range(DCH):
                j, slot = fw_item(
                    bwT[128 * dc:128 * (dc + 1), 256 * ob:256 * (ob + 1)], 256,
                    eng="v" if dc % 2 else "g")

                def _(te, j=j, slot=slot, ob=ob, dc=dc):
                    cwait(te, j)
                    if dc == 0:
                        te.wait_ge(sV.h, low_v)
                        te.wait_ge(sV.h, ev_v[ob - 2])
                    last = None
                    for o2 in range(2):
                        for qcx in range(NQC):
                            last = te.matmul(
                                ps_B[(ob % 2) * NB + o2 * NQC + qcx][:, 0:QCW],
                                fw_b[slot][:, 128 * o2:128 * (o2 + 1)],
                                x1f[:, dc * TOWN + qcx * QCW:
                                    dc * TOWN + (qcx + 1) * QCW],
                                start=(dc == 0), stop=False)
                    fst_done(j, sT.inc(last))
                block.tensor(_)

            j, slot = fw_item(lb_pk[0:ER, 256 * ob:256 * (ob + 1)], 256)

            def _(te, j=j, slot=slot, ob=ob):
                cwait(te, j)
                last = None
                for o2 in range(2):
                    for qcx in range(NQC):
                        last = te.matmul(
                            ps_B[(ob % 2) * NB + o2 * NQC + qcx][:, 0:QCW],
                            fw_b[slot][0:ER, 128 * o2:128 * (o2 + 1)],
                            low_bf[qcx][:, :], start=False, stop=True)
                fst_done(j, sT.inc(last))
                bw_done[ob] = sT.n
            block.tensor(_)

            if ob >= 1:
                emit_evac(ob - 1)
        emit_evac(NOB - 1)

        psB2b.close()
        psB2c = ExitStack()
        ps_st = [psB2c.enter_context(nc.psum_tensor(f"ps_st{i}", [128, 512], F32))
                 for i in range(2 * NQC)]
        ps_n1 = psB2c.enter_context(nc.psum_tensor("ps_n1", [128, 512], F32))
        ps_n2 = psB2c.enter_context(nc.psum_tensor("ps_n2", [128, 512], F32))

        sq_v = {}

        def _(dv):
            dv.wait_ge(sT.h, bw_done[NOB - 1])
            for dc in range(DCH):
                for qcx in range(NQC):
                    ch = slice(dc * TOWN + qcx * QCW, dc * TOWN + (qcx + 1) * QCW)
                    sq_v[(dc, qcx)] = sV.inc(dv.tensor_mul(
                        x1f[:, ch], R[:, ch], R[:, ch]))
        block.vector(_)

        st_done = [None] * NQC

        def _(te):
            te.wait_ge(sV.h, ev_v[NOB - 1])
            for dc in range(DCH):
                for qcx in range(NQC):
                    ch = slice(dc * TOWN + qcx * QCW, dc * TOWN + (qcx + 1) * QCW)
                    te.wait_ge(sV.h, sq_v[(dc, qcx)])
                    te.matmul(ps_st[2 * qcx][0:1, 0:QCW], ones_col[:, 0:1],
                              R[:, ch], start=(dc == 0), stop=(dc == DCH - 1))
                    last = te.matmul(ps_st[2 * qcx + 1][0:1, 0:QCW],
                                     ones_col[:, 0:1], x1f[:, ch],
                                     start=(dc == 0), stop=(dc == DCH - 1))
                    if dc == DCH - 1:
                        st_done[qcx] = sT.inc(last)
        block.tensor(_)

        SA = lnscr[0:1, 0:QCW]
        SB = lnscr[0:1, QCW:2 * QCW]
        SC2 = lnscr[0:1, 2 * QCW:3 * QCW]
        bcT = [None] * NQC
        out_cnt = 0
        prev_c2 = sC.n
        apv_prev = sV.n
        app_prev = sP.n
        for qcx in range(NQC):
            def _(dv, qcx=qcx, pc=prev_c2):
                dv.wait_ge(sT.h, st_done[qcx])
                dv.wait_ge(sC.h, pc)
                if qcx > 0:
                    dv.wait_ge(sT.h, bcT[qcx - 1])
                dv.tensor_scalar(SA, ps_st[2 * qcx + 1][0:1, 0:QCW], 1.0 / D,
                                 None, OP.mult)
                dv.tensor_scalar(SC2, ps_st[2 * qcx][0:1, 0:QCW], -1.0 / D,
                                 None, OP.mult)
                dv.engine_nop()
                dv.engine_nop()
                dv.tensor_mul(SB, SC2, SC2)
                sV.inc(dv.scalar_tensor_tensor(SA, SA, 1e-5, SB,
                                               OP.add, OP.subtract))
            block.vector(_)
            vw6 = sV.n

            def _(sc, vw6=vw6):
                sc.wait_ge(sV.h, vw6)
                sc.activation(SA, SA, AF.Ln)
                sC.inc(sc.activation(SB, SA, AF.Exp, scale=-0.5))
            block.scalar(_)
            cw2 = sC.n
            prev_c2 = cw2

            def _(te, qcx=qcx, cw2=cw2, apv=apv_prev, app=app_prev):
                te.wait_ge(sC.h, cw2)
                if qcx > 0:
                    te.wait_ge(sV.h, apv)
                    te.wait_ge(sP.h, app)
                te.matmul(ps_n1[:, 0:QCW], ones_row[0:1, :], SC2,
                          start=True, stop=True)
                bcT[qcx] = sT.inc(te.matmul(ps_n2[:, 0:QCW], ones_row[0:1, :],
                                            SB, start=True, stop=True))
            block.tensor(_)
            tw6 = sT.n

            def _(dv, tw6=tw6):
                dv.wait_ge(sT.h, tw6)
                dv.tensor_copy(out=NM2b[:, :], in_=ps_n1[:, 0:QCW])
                dv.tensor_copy(out=I2b[:, :], in_=ps_n2[:, 0:QCW])
                dv.engine_nop()
                sV.inc(dv.engine_nop())
            block.vector(_)
            bc2_v = sV.n

            for dc in range(DCH):
                ci = out_cnt + dc
                slot = ci % NOUT
                u = ci // NOUT
                ch = slice(dc * TOWN + qcx * QCW, dc * TOWN + (qcx + 1) * QCW)
                if ci % 4 == 3:
                    def _(g, slot=slot, u=u, ch=ch, bc2_v=bc2_v):
                        g.wait_ge(sV.h, bc2_v)
                        if u >= 1:
                            g.wait_ge(sOu[slot].h, 16 * u)
                        g.tensor_add(outst[slot][:, :], R[:, ch], NM2b[:, :])
                        sP.inc(g.tensor_mul(outst[slot][:, :],
                                            outst[slot][:, :], I2b[:, :]))
                    block.gpsimd(_)
                    prod = (sP.h, sP.n)
                    app_prev = sP.n
                else:
                    def _(dv, slot=slot, u=u, ch=ch):
                        if u >= 1:
                            dv.wait_ge(sOu[slot].h, 16 * u)
                        dv.tensor_add(outst[slot][:, :], R[:, ch], NM2b[:, :])
                        sV.inc(dv.tensor_mul(outst[slot][:, :],
                                             outst[slot][:, :], I2b[:, :]))
                    block.vector(_)
                    prod = (sV.h, sV.n)
                    apv_prev = sV.n

                def _(q, dc=dc, qcx=qcx, slot=slot, prod=prod):
                    q.wait_ge(prod[0], prod[1])
                    sOu[slot].n += 16
                    q.dma_start(out=out_dT[128 * dc:128 * (dc + 1),
                                           qcx * QCW:(qcx + 1) * QCW],
                                in_=outst[slot][:, :]).then_inc(sOu[slot].h, 16)
                block.sync(_)
            out_cnt += DCH

        psB2c.close()
        ctxB2.close()
        ctxB.close()

        def _(g):
            for s in sOu:
                if s.n:
                    g.wait_ge(s.h, s.n)
            if debug_dumps:
                sG.n += 16
                g.dma_start(out=dbg_y2[:, :], in_=R[:, :]).then_inc(sG.h, 16)
                g.wait_ge(sG.h, sG.n)
        block.gpsimd(_)
    return nc


def host_layout(inputs, D, NKV, TOWN, H):
    E = 64
    HE = H * E
    CCH = HE // 128
    DCH = D // 128
    x = np.asarray(inputs["L0_attn_x"], dtype=np.float32)
    wqk = np.asarray(inputs["L0_attn_Wqk_w"], dtype=np.float32)
    bqk = np.asarray(inputs["L0_attn_Wqk_b"], dtype=np.float32)
    wv = np.asarray(inputs["L0_attn_Wv_w"], dtype=np.float32)
    bv = np.asarray(inputs["L0_attn_Wv_b"], dtype=np.float32)
    wo = np.asarray(inputs["L0_attn_Wo_w"], dtype=np.float32)
    bo = np.asarray(inputs["L0_attn_Wo_b"], dtype=np.float32)
    gw = np.asarray(inputs["L0_ffn_gate_w"], dtype=np.float32)
    bw = np.asarray(inputs["L0_ffn_base_w"], dtype=np.float32)
    bb = np.asarray(inputs["L0_ffn_base_b"], dtype=np.float32)
    la = np.asarray(inputs["L0_ffn_lora_A"], dtype=np.float32)
    lb = np.asarray(inputs["L0_ffn_lora_B"], dtype=np.float32)

    C = np.ascontiguousarray
    shared = dict(
        wqT=C(wqk.T[:, 0::2]),
        wkT=C(wqk.T[:, 1::2]),
        bq_r=C(bqk[0::2].reshape(CCH, 128)),
        bk_r=C(bqk[1::2].reshape(CCH, 128)),
        wvT=C(wv.T),
        bv_r=C(bv.reshape(1, HE)),
        woT=C(wo.T),
        bo_r=C(bo.reshape(DCH, 128)),
        gw_pk=C(gw.T.reshape(DCH, 128, NE).transpose(1, 0, 2).reshape(128, NE * DCH)),
        bwT=C(bw.T),
        bb_r=C(bb.reshape(DCH, 128)),
        la_pk=C(la.reshape(ER, D).T.reshape(DCH, 128, ER).transpose(1, 0, 2)
                .reshape(128, ER * DCH)),
        lb_pk=C(lb.transpose(0, 2, 1).reshape(ER, D)),
    )
    in_maps = []
    for c in range(8):
        b = c // 4
        s = c % 4
        m = dict(shared)
        m["x_dT"] = C(x[b, TOWN * s:TOWN * (s + 1), :].T)
        in_maps.append(m)
    return in_maps


def kernel(**inputs):
    D, NKV, TOWN, H = 4096, 4096, 1024, 8
    nc = build_program(D, NKV, TOWN, H)
    in_maps = host_layout(inputs, D, NKV, TOWN, H)
    res = run_bass_kernel_spmd(nc, in_maps, core_ids=list(range(8)))
    out = np.empty((2, NKV, D), dtype=np.float32)
    for c in range(8):
        b, s = c // 4, c % 4
        out[b, TOWN * s:TOWN * (s + 1), :] = res.results[c]["out_dT"].T
    return out
